# version 13
# baseline (speedup 1.0000x reference)
import math
import contextlib
import numpy as np

import concourse.bass as bass
import concourse.mybir as mybir
from concourse.bass_utils import run_bass_kernel_spmd

F32 = mybir.dt.float32
BF16 = mybir.dt.bfloat16
I32 = mybir.dt.int32
AF = mybir.ActivationFunctionType
ALU = mybir.AluOpType
AX = mybir.AxisListType

B, S, D = 2, 8192, 2048
NH, DK = 16, 128
HL = 4
QT = 2048
NT = S
NTT = NT // 128
NFB = NT // 512
QBS = 1024
NQB = NT // QBS
E, RL = 8, 16
SCALE = 1.0 / math.sqrt(DK)
LORA_SCALING = 2.0

TEN, VEC, SCA, SYN, GPS = "tensor", "vector", "scalar", "sync", "gpsimd"


class Plan:
    def __init__(self):
        self.streams = {e: [] for e in (TEN, VEC, SCA, SYN, GPS)}
        self.ct = {}

    def op(self, eng, fn, waits=(), incs=()):
        for s, v in waits:
            have = self.ct.get(s, 0)
            assert v <= have, f"wait {s}>={v} but only {have} planned ({eng})"
        after = {}
        for s, n in incs:
            self.ct[s] = self.ct.get(s, 0) + n
            after[s] = self.ct[s]
        self.streams[eng].append((fn, tuple(waits), tuple(incs)))
        return after

    def cur(self, s):
        return self.ct.get(s, 0)


def build_nc(debug=False):
    nc = bass.Bass("TRN2", target_bir_lowering=False)

    x_in = nc.dram_tensor("x", [NT, D], F32, kind="ExternalInput").ap()
    xq_in = nc.dram_tensor("xq", [QT, D], F32, kind="ExternalInput").ap()
    wqT = nc.dram_tensor("wqT", [D, HL * DK], F32, kind="ExternalInput").ap()
    wkT = nc.dram_tensor("wkT", [D, HL * DK], F32, kind="ExternalInput").ap()
    wvT = nc.dram_tensor("wvT", [D, HL * DK], F32, kind="ExternalInput").ap()
    woT = nc.dram_tensor("woT", [D, D], F32, kind="ExternalInput").ap()
    bwT = nc.dram_tensor("bwT", [D, D], F32, kind="ExternalInput").ap()
    bb_in = nc.dram_tensor("bb", [128, D], F32, kind="ExternalInput").ap()
    rwT = nc.dram_tensor("rwT", [D, E], F32, kind="ExternalInput").ap()
    rb_in = nc.dram_tensor("rb", [128, E], F32, kind="ExternalInput").ap()
    aall = nc.dram_tensor("aall", [D, E * RL], F32, kind="ExternalInput").ap()
    ball = nc.dram_tensor("ball", [E * RL, D], F32, kind="ExternalInput").ap()
    ropec = nc.dram_tensor("ropec", [DK, NT], BF16, kind="ExternalInput").ap()
    ropes = nc.dram_tensor("ropes", [DK, NT], BF16, kind="ExternalInput").ap()
    dmask_in = nc.dram_tensor("dmask", [128, 8 * QBS], BF16, kind="ExternalInput").ap()
    pswap_in = nc.dram_tensor("pswap", [128, 128], BF16, kind="ExternalInput").ap()
    rowoff_in = nc.dram_tensor("rowoff", [1, 1], I32, kind="ExternalInput").ap()
    out = nc.dram_tensor("out", [QT, D], F32, kind="ExternalOutput").ap()

    xn_hbm = nc.dram_tensor("xn_hbm", [NT, D], BF16, kind="Internal").ap()
    q_spill = nc.dram_tensor("q_spill", [HL * DK, NT], BF16, kind="Internal").ap()
    k_spill = nc.dram_tensor("k_spill", [HL * DK, NT], BF16, kind="Internal").ap()
    v_spill = nc.dram_tensor("v_spill", [HL * NT, DK], BF16, kind="Internal").ap()
    ag_in_h = [nc.dram_tensor(f"ag_in{h}", [NT, DK], BF16, kind="Internal").ap()
               for h in range(HL)]
    ag_out_h = [nc.dram_tensor(f"ag_out{h}", [4 * NT, DK], BF16, kind="Internal").ap()
                for h in range(HL)]
    ctx_mine = nc.dram_tensor("ctx_mine", [4 * QT, HL * DK], BF16, kind="Internal").ap()
    x2_hbm = nc.dram_tensor("x2_hbm", [QT, D], F32, kind="Internal").ap()
    xn2_hbm = nc.dram_tensor("xn2_hbm", [QT, D], BF16, kind="Internal").ap()
    midg_hbm = nc.dram_tensor("midg_hbm", [QT, E * RL], BF16, kind="Internal").ap()

    P = Plan()
    es = contextlib.ExitStack()

    AR1 = es.enter_context(nc.sbuf_tensor("AR1", [128, 75776], BF16))
    AR2 = es.enter_context(nc.sbuf_tensor("AR2", [128, 8192], F32))

    def a1(off, n):
        return AR1[:, off:off + n]

    def a2(off, n):
        return AR2[:, off:off + n]

    bb_sb = es.enter_context(nc.sbuf_tensor("bb_sb", [128, D], F32))
    rb_sb = es.enter_context(nc.sbuf_tensor("rb_sb", [128, E], F32))
    pswap = es.enter_context(nc.sbuf_tensor("pswap_sb", [128, 128], BF16))
    eps_sb = es.enter_context(nc.sbuf_tensor("eps_sb", [128, 1], F32))
    stats = es.enter_context(nc.sbuf_tensor("stats", [128, 2 * 24], F32))
    mv = es.enter_context(nc.sbuf_tensor("mv", [128, 2 * 2], F32))
    lnt = es.enter_context(nc.sbuf_tensor("lnt", [128, 2], F32))
    rstd = es.enter_context(nc.sbuf_tensor("rstd", [128, 2], F32))
    mv8 = es.enter_context(nc.sbuf_tensor("mv8", [128, 2 * 16], F32))
    lnt8 = es.enter_context(nc.sbuf_tensor("lnt8", [128, 2 * 8], F32))
    rstd8 = es.enter_context(nc.sbuf_tensor("rstd8", [128, 2 * 8], F32))
    bias8 = es.enter_context(nc.sbuf_tensor("bias8", [128, 2 * 8], F32))
    recip = es.enter_context(nc.sbuf_tensor("recip", [128, 8], F32))
    glog = es.enter_context(nc.sbuf_tensor("glog", [128, E], F32))
    gnm = es.enter_context(nc.sbuf_tensor("gnm", [128, 1], F32))
    gu = es.enter_context(nc.sbuf_tensor("gu", [128, E], F32))
    gm1 = es.enter_context(nc.sbuf_tensor("gm1", [128, 1], F32))
    gmk1 = es.enter_context(nc.sbuf_tensor("gmk1", [128, E], F32))
    gu2 = es.enter_context(nc.sbuf_tensor("gu2", [128, E], F32))
    gm2 = es.enter_context(nc.sbuf_tensor("gm2", [128, 1], F32))
    gmk2 = es.enter_context(nc.sbuf_tensor("gmk2", [128, E], F32))
    ggu = es.enter_context(nc.sbuf_tensor("ggu", [128, E], F32))
    gsum = es.enter_context(nc.sbuf_tensor("gsum", [128, 1], F32))
    grec = es.enter_context(nc.sbuf_tensor("grec", [128, 1], F32))
    gate = es.enter_context(nc.sbuf_tensor("gate", [128, E], F32))
    roff_sb = es.enter_context(nc.sbuf_tensor("roff_sb", [1, 1], I32))
    dmask = es.enter_context(nc.sbuf_tensor("dmask_sb", [128, 8 * QBS], BF16))

    PS = es.enter_context(nc.psum_tensor("PS", [128, 4096], F32))

    def bank(i, n=512):
        return PS[:, 512 * i: 512 * i + n]

    sem = {}
    for name in ("ld", "gd", "mm", "vc", "ac", "cc"):
        sem[name] = es.enter_context(nc.semaphore(name))

    WQ, WK, WV = 0, 8192, 16384
    XND = (24576, 32768, 40960, 49152)
    XNT = (57344, 59392)
    QEV = (61440, 62464)
    QSW = (63488, 64512)
    RT1, RT2 = 65536, 66560
    QR = (67584, 68608)
    VST = (69632, 70144)
    RPC = (70656, 71680)
    RPS = (72704, 73728)
    KBF = (0, 8192)
    VBF = (16384, 24832)
    QBF = (33280, 34304)
    PTS = (35328, 36352)
    CTXS = (37376, 38400)
    WOT = 40960
    CTXD = 0
    XN2T = (32768, 34816)
    BWT = 40960
    XN2D = (0, 8192)
    AALL = 16384
    BALL = 18432
    RWT = 20480
    MIDG = 20608
    MIDGD = 20736
    XT = (0, 2048)
    X2T = (4096, 6144)

    for dst, src in ((pswap[:, :], pswap_in[:, :]), (bb_sb[:, :], bb_in[:, :]),
                     (rb_sb[:, :], rb_in[:, :]), (dmask[:, :], dmask_in[:, :])):
        P.op(SYN, (lambda d, s: (lambda _nc: _nc.sync.dma_start(out=d, in_=s)))(dst, src),
             incs=[("ld", 16)])
    for off, src in ((WQ, wqT), (WK, wkT), (WV, wvT)):
        dst = a1(off, 8192).rearrange("p (c k) -> p c k", k=512)
        srcr = src.rearrange("(c p) k -> p c k", p=128)
        P.op(GPS, (lambda d, s: (lambda _nc: _nc.gpsimd.dma_start(out=d, in_=s)))(dst, srcr),
             incs=[("gd", 16)])
    P.op(GPS, lambda _nc: _nc.gpsimd.dma_start(out=roff_sb[:, :], in_=rowoff_in[:, :]),
         incs=[("gd", 16)])
    ld_p0 = P.cur("ld")
    gd_p0 = P.cur("gd")
    P.op(VEC, lambda _nc: _nc.vector.memset(eps_sb[:, :], 1e-5), incs=[("vc", 1)])

    def _mk_rowoff(_nc, _ctx):
        reg = _nc.gpsimd.alloc_register("rowoff_reg")
        _nc.gpsimd.reg_load(reg, roff_sb[0:1, 0:1])
        _ctx["rowoff"] = _nc.gpsimd.snap(reg, min_val=0, max_val=3 * QT)
    P.op(GPS, ("ctx", _mk_rowoff), waits=[("gd", gd_p0)])

    NDB = NFB // 2
    xn_wr_gd = [0] * NTT
    xt_free = [0, 0, 0, 0]
    xn_free = [0, 0]

    def make_ln_steps(db):
        dpar = db % 2
        mv8v = mv8.ap().rearrange("p (d k two) -> p d two k", d=2, two=2)
        ld_tile = [0] * 8

        def emit_x_load(i8):
            i = 8 * db + i8
            par = i % 4
            w = [("vc", xt_free[par])] if xt_free[par] else []
            ld_tile[i8] = P.op(SYN, (lambda i_, p_: (lambda _nc: _nc.sync.dma_start(
                out=a2(2048 * p_, 2048), in_=x_in[128 * i_:128 * (i_ + 1), :])))(i, par),
                waits=w, incs=[("ld", 16)])["ld"]

        for i8 in range(4):
            emit_x_load(i8)

        def step(pr):
            for j in range(2):
                i8 = 2 * pr + j
                par = (8 * db + i8) % 4
                spar = (8 * db + i8) % 2
                for c in range(4):
                    P.op(VEC, (lambda p_, sp_, c_: (lambda _nc: _nc.vector.bn_stats(
                        stats[:, 24 * sp_ + 6 * c_:24 * sp_ + 6 * (c_ + 1)],
                        a2(2048 * p_ + 512 * c_, 512))))(par, spar, c),
                        waits=[("ld", ld_tile[i8])] if c == 0 else [],
                        incs=[("vc", 1)] if c == 3 else [])
                vbn = P.cur("vc")
                P.op(VEC, (lambda sp_, d_, k_: (lambda _nc: _nc.vector.bn_aggr(
                    mv8[:, 16 * d_ + 2 * k_:16 * d_ + 2 * k_ + 2],
                    stats[:, 24 * sp_:24 * sp_ + 24])))(spar, dpar, i8),
                    waits=[("vc", vbn)], incs=[("vc", 1)])
            vagg = P.cur("vc")
            aln = P.op(SCA, (lambda d_, r_, vv: (lambda _nc: _nc.scalar.activation(
                lnt8[:, 8 * d_ + 2 * r_:8 * d_ + 2 * r_ + 2],
                vv[:, d_, 1, 2 * r_:2 * r_ + 2], AF.Ln,
                bias=eps_sb[:, 0:1])))(dpar, pr, mv8v), waits=[("vc", vagg)],
                incs=[("ac", 1)])["ac"]
            aex = P.op(SCA, (lambda d_, r_: (lambda _nc: _nc.scalar.activation(
                rstd8[:, 8 * d_ + 2 * r_:8 * d_ + 2 * r_ + 2],
                lnt8[:, 8 * d_ + 2 * r_:8 * d_ + 2 * r_ + 2], AF.Exp,
                scale=-0.5)))(dpar, pr), waits=[("ac", aln)], incs=[("ac", 1)])["ac"]
            for j in range(2):
                i8 = 2 * pr + j
                i = 8 * db + i8
                par = i % 4
                spar = i % 2
                wts = [("ac", aex)]
                if xn_free[spar]:
                    wts.append(("gd", xn_free[spar]))
                vno = P.op(VEC, (lambda p_, sp_, d_, k_, vv: (lambda _nc:
                    _nc.vector.tensor_scalar(
                        a1(XNT[sp_], 2048), a2(2048 * p_, 2048),
                        vv[:, d_, 0, k_:k_ + 1],
                        rstd8[:, 8 * d_ + k_:8 * d_ + k_ + 1],
                        ALU.subtract, ALU.mult)))(par, spar, dpar, i8, mv8v),
                    waits=wts, incs=[("vc", 1)])["vc"]
                xt_free[par] = vno
                g = P.op(GPS, (lambda i_, sp_: (lambda _nc: _nc.gpsimd.dma_start(
                    out=xn_hbm[128 * i_:128 * (i_ + 1), :],
                    in_=a1(XNT[sp_], 2048))))(i, spar),
                    waits=[("vc", vno)], incs=[("gd", 16)])["gd"]
                xn_wr_gd[i] = g
                xn_free[spar] = g
            for j in range(2):
                if 2 * pr + 4 + j < 8:
                    emit_x_load(2 * pr + 4 + j)

        return [lambda pr=pr: step(pr) for pr in range(4)]

    xnd_rd_ld = [0] * NDB
    xnd_users_mm = [0, 0]
    rope_tab_ld = [0, 0]
    psq_free_ac = [0, 0]
    psw_free_ac = 0
    psv_free_ac = [0, 0]
    qev_free_mm = [0, 0]
    qev_free_vc = [0, 0]
    qsw_free_vc = [0, 0]
    qr_free_gd = [0, 0]
    vst_free_gd = [0, 0]
    rope_tab_users_vc = [0, 0]

    def emit_block_reads(db):
        par = db % 2
        w = [("gd", max(xn_wr_gd[8 * db + k] for k in range(8)))]
        if xnd_users_mm[par]:
            w.append(("mm", xnd_users_mm[par]))
        base = XND[2 * par]
        last = 0
        for c in range(16):
            last = P.op(SYN, (lambda d_, b_, c_: (lambda _nc: _nc.sync.dma_start_transpose(
                a1(b_ + 1024 * c_, 1024),
                xn_hbm[1024 * d_:1024 * (d_ + 1), 128 * c_:128 * (c_ + 1)])))(db, base, c),
                waits=w if c == 0 else [], incs=[("ld", 16)])["ld"]
        xnd_rd_ld[db] = last

    def emit_rope_table_loads(db):
        dpar = db % 2
        wts = []
        if rope_tab_users_vc[dpar]:
            wts.append(("gd", rope_tab_users_vc[dpar]))
        P.op(SYN, (lambda d_, p_: (lambda _nc: _nc.sync.dma_start(
            out=a1(RPC[p_], 1024), in_=ropec[:, QBS * d_:QBS * (d_ + 1)])))(db, dpar),
            waits=wts, incs=[("ld", 16)])
        rope_tab_ld[dpar] = P.op(SYN, (lambda d_, p_: (lambda _nc: _nc.sync.dma_start(
            out=a1(RPS[p_], 1024), in_=ropes[:, QBS * d_:QBS * (d_ + 1)])))(db, dpar),
            incs=[("ld", 16)])["ld"]

    def emit_proj_db(db, ln_steps):
        nonlocal psw_free_ac
        dpar = db % 2
        base = XND[2 * dpar]
        R = a1(base, 16384).rearrange("p (c b k) -> p c b k", b=2, k=512)
        rd_ld = xnd_rd_ld[db]
        jobs = [(kind, h) for h in range(HL) for kind in ("q", "k")]

        def emit_mms(ji):
            kind, h = jobs[ji]
            par = ji % 2
            woff = WQ if kind == "q" else WK
            wview = a1(woff, 8192).rearrange("p (c k) -> p c k", k=512)
            wts = [("ld", rd_ld), ("gd", gd_p0)]
            if psq_free_ac[par]:
                wts.append(("ac", psq_free_ac[par]))
            for c in range(16):
                for half in range(2):
                    P.op(TEN, (lambda p_, c_, hf, wv, h_: (lambda _nc: _nc.tensor.matmul(
                        PS[:, 1024 * p_ + 512 * hf:1024 * p_ + 512 * (hf + 1)],
                        wv[:, c_, 128 * h_:128 * (h_ + 1)], R[:, c_, hf, :],
                        start=(c_ == 0), stop=(c_ == 15))))(par, c, half, wview, h),
                        waits=wts if (c == 0 and half == 0) else [],
                        incs=[("mm", 1)] if (c == 15 and half == 1) else [])
            return P.cur("mm")

        def emit_evict(ji, mmv):
            par = ji % 2
            wts = [("mm", mmv)]
            if qev_free_mm[par]:
                wts.append(("mm", qev_free_mm[par]))
            if qev_free_vc[par]:
                wts.append(("gd", qev_free_vc[par]))
            a = P.op(SCA, (lambda p_: (lambda _nc: _nc.scalar.copy(
                a1(QEV[p_], 1024), PS[:, 1024 * p_:1024 * (p_ + 1)])))(par),
                waits=wts, incs=[("ac", 1)])["ac"]
            psq_free_ac[par] = a
            return a

        def emit_swap(ji, acv):
            nonlocal psw_free_ac
            par = ji % 2
            wts = [("ac", acv), ("ld", ld_p0)]
            if psw_free_ac:
                wts.append(("ac", psw_free_ac))
            for half in range(2):
                P.op(TEN, (lambda p_, hf: (lambda _nc: _nc.tensor.matmul(
                    PS[:, 2048 + 512 * hf:2048 + 512 * (hf + 1)], pswap[:, :],
                    a1(QEV[p_], 1024)[:, 512 * hf:512 * (hf + 1)],
                    start=True, stop=True)))(par, half),
                    waits=wts if half == 0 else [],
                    incs=[("mm", 1)] if half == 1 else [])
            mmv = P.cur("mm")
            wts = [("mm", mmv)]
            if qsw_free_vc[par]:
                wts.append(("gd", qsw_free_vc[par]))
            a = P.op(SCA, (lambda p_: (lambda _nc: _nc.scalar.copy(
                a1(QSW[p_], 1024), PS[:, 2048:3072])))(par), waits=wts,
                incs=[("ac", 1)])["ac"]
            psw_free_ac = a
            return a

        def emit_rope(ji, acv):
            kind, h = jobs[ji]
            par = ji % 2
            P.op(GPS, (lambda p_, d_: (lambda _nc: _nc.gpsimd.tensor_mul(
                a1(RT1, 1024), a1(QEV[p_], 1024), a1(RPC[d_], 1024))))(par, dpar),
                waits=[("ac", acv), ("ld", rope_tab_ld[dpar])])
            v2 = P.op(GPS, (lambda p_, d_: (lambda _nc: _nc.gpsimd.tensor_mul(
                a1(RT2, 1024), a1(QSW[p_], 1024), a1(RPS[d_], 1024))))(par, dpar),
                incs=[("gd", 1)])["gd"]
            qev_free_vc[par] = v2
            qsw_free_vc[par] = v2
            wts = [("gd", max(v2, qr_free_gd[par]))]
            v3 = P.op(GPS, (lambda p_: (lambda _nc: _nc.gpsimd.tensor_add(
                a1(QR[p_], 1024), a1(RT1, 1024), a1(RT2, 1024))))(par),
                waits=wts, incs=[("gd", 1)])["gd"]
            rope_tab_users_vc[dpar] = v3
            spill = q_spill if kind == "q" else k_spill
            g = P.op(GPS, (lambda p_, sp, h_, d_: (lambda _nc: _nc.gpsimd.dma_start(
                out=sp[128 * h_:128 * (h_ + 1), QBS * d_:QBS * (d_ + 1)],
                in_=a1(QR[p_], 1024))))(par, spill, h, db), waits=[("gd", v3)],
                incs=[("gd", 16)])["gd"]
            qr_free_gd[par] = g

        pend = []
        for ji in range(len(jobs)):
            m = emit_mms(ji)
            if pend:
                pj, pm = pend.pop()
                a = emit_evict(pj, pm)
                sa = emit_swap(pj, a)
                emit_rope(pj, sa)
            pend.append((ji, m))
            if ji % 2 == 1 and ln_steps[ji // 2] is not None:
                ln_steps[ji // 2]()
        pj, pm = pend.pop()
        a = emit_evict(pj, pm)
        sa = emit_swap(pj, a)
        emit_rope(pj, sa)

        wvview = a1(WV, 8192).rearrange("p (c k) -> p c k", k=512)
        for tt in range(8):
            par = tt % 2
            wts = [("ld", rd_ld), ("gd", gd_p0)]
            if psv_free_ac[par]:
                wts.append(("ac", psv_free_ac[par]))
            for c in range(16):
                P.op(TEN, (lambda p_, c_, t_, wv: (lambda _nc: _nc.tensor.matmul(
                    PS[:, 3072 + 512 * p_:3072 + 512 * (p_ + 1)],
                    R[:, c_, t_ // 4, 128 * (t_ % 4):128 * (t_ % 4 + 1)], wv[:, c_, :],
                    start=(c_ == 0), stop=(c_ == 15))))(par, c, tt, wvview),
                    waits=wts if c == 0 else [], incs=[("mm", 1)] if c == 15 else [])
            mmv = P.cur("mm")
            wts = [("mm", mmv)]
            if vst_free_gd[par]:
                wts.append(("gd", vst_free_gd[par]))
            av = P.op(SCA, (lambda p_: (lambda _nc: _nc.scalar.copy(
                a1(VST[p_], 512), PS[:, 3072 + 512 * p_:3072 + 512 * (p_ + 1)])))(par),
                waits=wts, incs=[("ac", 1)])["ac"]
            psv_free_ac[par] = av
            trow = QBS * db + 128 * tt
            vdst = v_spill.rearrange("(h t) d -> h t d", h=HL)[
                :, trow:trow + 128, :].rearrange("h p d -> p h d")
            vsrc = a1(VST[par], 512).rearrange("p (h d) -> p h d", d=DK)
            g = P.op(GPS, (lambda dd, ss: (lambda _nc: _nc.gpsimd.dma_start(
                out=dd, in_=ss)))(vdst, vsrc), waits=[("ac", av)], incs=[("gd", 16)])["gd"]
            vst_free_gd[par] = g
        xnd_users_mm[dpar] = P.cur("mm")

    for db0 in (0, 1):
        for s in make_ln_steps(db0):
            s()
    emit_rope_table_loads(0)
    for db in range(NDB):
        emit_block_reads(db)
        if db + 1 < NDB:
            emit_rope_table_loads(db + 1)
        steps = make_ln_steps(db + 2) if db + 2 < NDB else [None] * 4
        emit_proj_db(db, steps)

    p1_mm = P.cur("mm")
    p1_vc = P.cur("vc")
    p1_gd = P.cur("gd")

    P.op(GPS, lambda _nc: _nc.gpsimd.dma_start(
        out=a1(WOT, 32768).rearrange("p (c o) -> p c o", o=2048),
        in_=woT.rearrange("(c p) o -> p c o", p=128)),
        waits=[("mm", p1_mm), ("vc", p1_vc)], incs=[("gd", 16)])
    wot_gd = P.cur("gd")

    kv_ld = [0, 0]
    kv_free_mm = [0, 0]
    vones_vc = [0, 0]

    def emit_kv_load(h):
        par = h % 2
        wts = [("gd", p1_gd)]
        if kv_free_mm[par]:
            wts.append(("mm", kv_free_mm[par]))
        P.op(SYN, (lambda p_, h_: (lambda _nc: _nc.sync.dma_start(
            out=a1(KBF[p_], 8192), in_=k_spill[128 * h_:128 * (h_ + 1), :])))(par, h),
            waits=wts, incs=[("ld", 16)])
        vdst = a1(VBF[par], 8448).rearrange("p (n c) -> p n c", c=132)[:, :, 0:128]
        vsrc = v_spill[NT * h:NT * (h + 1), :].rearrange("(n p) d -> p n d", p=128)
        P.op(SYN, (lambda dd, ss: (lambda _nc: _nc.sync.dma_start(out=dd, in_=ss)))(
            vdst, vsrc), incs=[("ld", 16)])
        kv_ld[par] = P.cur("ld")
        vcols = a1(VBF[par], 8448).rearrange("p (n c) -> p n c", c=132)[:, :, 128:129]
        vones_vc[par] = P.op(VEC, (lambda vv: (lambda _nc: _nc.vector.memset(vv, 1.0)))(
            vcols), waits=[("ld", kv_ld[par])], incs=[("vc", 1)])["vc"]

    qld = [0, 0]
    q_free_mm = [0, 0]
    pt_ps_free = [0, 0]
    pt_sb_free = [0, 0]
    ctxs_free_gd = [0, 0]
    ctx_evict_vc = 0
    chunk_ctr = 0

    emit_kv_load(0)

    for h in range(HL):
        hpar = h % 2
        if h + 1 < HL:
            emit_kv_load(h + 1)
        for qb in range(NQB):
            qpar = qb % 2
            wts = [("gd", p1_gd)]
            if q_free_mm[qpar]:
                wts.append(("mm", q_free_mm[qpar]))
            qld[qpar] = P.op(SYN, (lambda p_, h_, q_: (lambda _nc: _nc.sync.dma_start(
                out=a1(QBF[p_], 1024),
                in_=q_spill[128 * h_:128 * (h_ + 1), QBS * q_:QBS * (q_ + 1)])))(
                qpar, h, qb), waits=wts, incs=[("ld", 16)])["ld"]
            nch = 8 * (qb + 1)
            exp_ac = {}
            mask_vc = {}

            def emit_sT(j, nch_, qpar_, hpar_):
                nonlocal chunk_ctr
                par = chunk_ctr % 2
                wts = [("ld", max(qld[qpar_], kv_ld[hpar_])), ("vc", vones_vc[hpar_])]
                if pt_ps_free[par]:
                    wts.append(("ac", pt_ps_free[par]))
                for half in range(2):
                    P.op(TEN, (lambda p_, j_, hf, hp, qp: (lambda _nc: _nc.tensor.matmul(
                        PS[:, 1024 * p_ + 512 * hf: 1024 * p_ + 512 * (hf + 1)],
                        a1(KBF[hp], 8192)[:, 128 * j_:128 * (j_ + 1)],
                        a1(QBF[qp], 1024)[:, 512 * hf:512 * (hf + 1)],
                        start=True, stop=True)))(par, j, half, hpar_, qpar_),
                        waits=wts if half == 0 else [],
                        incs=[("mm", 1)] if half == 1 else [])
                w2 = [("mm", P.cur("mm"))]
                if pt_sb_free[par]:
                    w2.append(("mm", pt_sb_free[par]))
                a = P.op(SCA, (lambda p_: (lambda _nc: _nc.scalar.activation(
                    a1(PTS[p_], 1024), PS[:, 1024 * p_:1024 * (p_ + 1)],
                    AF.Exp, scale=SCALE)))(par), waits=w2, incs=[("ac", 1)])["ac"]
                pt_ps_free[par] = a
                exp_ac[j] = a
                if j >= nch_ - 8:
                    jj = j - (nch_ - 8)
                    v = P.op(VEC, (lambda p_, jj_: (lambda _nc: _nc.vector.tensor_mul(
                        a1(PTS[p_], 1024), a1(PTS[p_], 1024),
                        dmask[:, QBS * jj_:QBS * (jj_ + 1)])))(par, jj),
                        waits=[("ac", a), ("ld", ld_p0)], incs=[("vc", 1)])["vc"]
                    mask_vc[j] = v
                chunk_ctr += 1
                return chunk_ctr - 1

            def emit_ctx(j, gidx, nch_, hpar_):
                par = gidx % 2
                if j in mask_vc:
                    wts = [("vc", mask_vc[j])]
                else:
                    wts = [("ac", exp_ac[j])]
                if j == 0 and ctx_evict_vc:
                    wts.append(("vc", ctx_evict_vc))
                vview = a1(VBF[hpar_], 8448).rearrange("p (n c) -> p n c", c=132)
                for s_ in range(8):
                    P.op(TEN, (lambda p_, j_, ss, vv, n_: (lambda _nc: _nc.tensor.matmul(
                        PS[:, 2048 + 256 * ss: 2048 + 256 * ss + 129],
                        a1(PTS[p_], 1024)[:, 128 * ss:128 * (ss + 1)],
                        vv[:, j_, 0:129],
                        start=(j_ == 0 and ss % 2 == 0),
                        stop=(j_ == n_ - 1))))(par, j, s_, vview, nch_),
                        waits=wts if s_ == 0 else [],
                        incs=[("mm", 1)] if s_ == 7 else [])
                return P.cur("mm")

            prev = None
            for j in range(nch):
                g = emit_sT(j, nch, qpar, hpar)
                if prev is not None:
                    m = emit_ctx(prev[0], prev[1], nch, hpar)
                    pt_sb_free[prev[1] % 2] = m
                prev = (j, g)
            m = emit_ctx(prev[0], prev[1], nch, hpar)
            pt_sb_free[prev[1] % 2] = m
            ctx_done = m
            for s_ in range(8):
                P.op(VEC, (lambda ss: (lambda _nc: _nc.vector.reciprocal(
                    recip[:, ss:ss + 1],
                    PS[:, 2048 + 256 * ss + 128:2048 + 256 * ss + 129])))(s_),
                    waits=[("mm", ctx_done)] if s_ == 0 else [],
                    incs=[("vc", 1)] if s_ == 7 else [])
            vrec = P.cur("vc")
            wts = [("vc", vrec)]
            if ctxs_free_gd[qpar]:
                wts.append(("gd", ctxs_free_gd[qpar]))
            for s_ in range(8):
                P.op(VEC, (lambda p_, ss: (lambda _nc: _nc.vector.tensor_scalar_mul(
                    a1(CTXS[p_], 1024)[:, 128 * ss:128 * (ss + 1)],
                    PS[:, 2048 + 256 * ss:2048 + 256 * ss + 128],
                    recip[:, ss:ss + 1])))(qpar, s_),
                    waits=wts if s_ == 0 else [], incs=[("vc", 1)] if s_ == 7 else [])
            ctx_evict_vc = P.cur("vc")
            agdst = ag_in_h[h][QBS * qb:QBS * (qb + 1), :].rearrange(
                "(s p) d -> p s d", p=128)
            agsrc = a1(CTXS[qpar], 1024).rearrange("p (s d) -> p s d", d=128)
            g2 = P.op(GPS, (lambda dd, ss: (lambda _nc: _nc.gpsimd.dma_start(
                out=dd, in_=ss)))(agdst, agsrc), waits=[("vc", ctx_evict_vc)],
                incs=[("gd", 16)])["gd"]
            ctxs_free_gd[qpar] = g2
            q_free_mm[qpar] = ctx_done
        P.op(GPS, (lambda h_: (lambda _nc: _nc.gpsimd.collective_compute(
            "AllGather", ALU.bypass, ins=[ag_in_h[h_].opt()],
            outs=[ag_out_h[h_].opt()],
            replica_groups=[[0, 1, 2, 3], [4, 5, 6, 7]])))(h),
            waits=[("gd", P.cur("gd"))], incs=[("cc", 1)])
        cch = P.cur("cc")
        for p_ in range(4):
            def _mk_gather(pp, hh):
                def f(_nc, _ctx):
                    v = _ctx["rowoff"]
                    return _nc.gpsimd.dma_start(
                        out=ctx_mine[QT * pp:QT * (pp + 1), 128 * hh:128 * (hh + 1)],
                        in_=ag_out_h[hh][NT * pp:NT * (pp + 1), :][bass.ds(v, QT), :])
                return f
            P.op(GPS, ("ctx", _mk_gather(p_, h)), waits=[("cc", cch)] if p_ == 0 else [],
                 incs=[("gd", 16)])
        kv_free_mm[hpar] = P.cur("mm")

    p2_gd = P.cur("gd")
    p2_mm = P.cur("mm")
    p2_vc = P.cur("vc")
    gather_gd = P.cur("gd")
    for ci in range(16):
        pq, hl_ = ci // 4, ci % 4
        P.op(SYN, (lambda p_, h_, c_: (lambda _nc: _nc.sync.dma_start_transpose(
            a1(CTXD + 2048 * c_, 2048),
            ctx_mine[QT * p_:QT * (p_ + 1), 128 * h_:128 * (h_ + 1)])))(pq, hl_, ci),
            waits=[("gd", gather_gd), ("mm", p2_mm), ("vc", p2_vc)] if ci == 0 else [],
            incs=[("ld", 16)])
    ctxd_ld = P.cur("ld")

    xq_free_vc = [0, 0]
    x2_free = [0, 0]
    op_ps_free = [0, 0]
    xn2_gd = [0] * 16
    x2_gd = [0] * 16

    wot_view = a1(WOT, 32768).rearrange("p (c o) -> p c o", o=2048)
    ctxd_view = a1(CTXD, 32768).rearrange("p (c t) -> p c t", t=2048)

    for t in range(16):
        par = t % 2
        wts = []
        if xq_free_vc[par]:
            wts.append(("vc", xq_free_vc[par]))
        xq_ld = P.op(SYN, (lambda t_, p_: (lambda _nc: _nc.sync.dma_start(
            out=a2(XT[p_], 2048), in_=xq_in[128 * t_:128 * (t_ + 1), :])))(t, par),
            waits=wts, incs=[("ld", 16)])["ld"]
        for oc in range(4):
            bk = 4 * par + oc
            wts = [("ld", ctxd_ld), ("gd", wot_gd)]
            if op_ps_free[par]:
                wts.append(("vc", op_ps_free[par]))
            for hdc in range(16):
                P.op(TEN, (lambda b_, h_, o_, t_: (lambda _nc: _nc.tensor.matmul(
                    bank(b_), ctxd_view[:, h_, 128 * t_:128 * (t_ + 1)],
                    wot_view[:, h_, 512 * o_:512 * (o_ + 1)],
                    start=(h_ == 0), stop=(h_ == 15))))(bk, hdc, oc, t),
                    waits=wts if (hdc == 0 and oc == 0) else [],
                    incs=[("mm", 1)] if hdc == 15 else [])
        mmv = P.cur("mm")
        wts = [("mm", mmv), ("ld", xq_ld)]
        if x2_free[par]:
            wts.append(("gd", x2_free[par]))
        for oc in range(4):
            bk = 4 * par + oc
            P.op(VEC, (lambda b_, p_, o_: (lambda _nc: _nc.vector.tensor_add(
                a2(X2T[p_], 2048)[:, 512 * o_:512 * (o_ + 1)], bank(b_),
                a2(XT[p_], 2048)[:, 512 * o_:512 * (o_ + 1)])))(bk, par, oc),
                waits=wts if oc == 0 else [], incs=[("vc", 1)] if oc == 3 else [])
        vadd = P.cur("vc")
        op_ps_free[par] = vadd
        xq_free_vc[par] = vadd
        for c in range(4):
            P.op(VEC, (lambda p_, c_: (lambda _nc: _nc.vector.bn_stats(
                stats[:, 24 * p_ + 6 * c_:24 * p_ + 6 * (c_ + 1)],
                a2(X2T[p_] + 512 * c_, 512))))(par, c),
                waits=[("vc", vadd)] if c == 0 else [],
                incs=[("vc", 1)] if c == 3 else [])
        vbn = P.cur("vc")
        vag = P.op(VEC, (lambda p_: (lambda _nc: _nc.vector.bn_aggr(
            mv[:, 2 * p_:2 * p_ + 2], stats[:, 24 * p_:24 * p_ + 24])))(par),
            waits=[("vc", vbn)], incs=[("vc", 1)])["vc"]
        aln = P.op(SCA, (lambda p_: (lambda _nc: _nc.scalar.activation(
            lnt[:, p_:p_ + 1], mv[:, 2 * p_ + 1:2 * p_ + 2], AF.Ln,
            bias=eps_sb[:, 0:1])))(par), waits=[("vc", vag)], incs=[("ac", 1)])["ac"]
        aex = P.op(SCA, (lambda p_: (lambda _nc: _nc.scalar.activation(
            rstd[:, p_:p_ + 1], lnt[:, p_:p_ + 1], AF.Exp, scale=-0.5)))(par),
            waits=[("ac", aln)], incs=[("ac", 1)])["ac"]
        vno = P.op(VEC, (lambda p_: (lambda _nc: _nc.vector.tensor_scalar(
            a1(XN2T[p_], 2048), a2(X2T[p_], 2048), mv[:, 2 * p_:2 * p_ + 1],
            rstd[:, p_:p_ + 1], ALU.subtract, ALU.mult)))(par),
            waits=[("ac", aex)], incs=[("vc", 1)])["vc"]
        P.op(GPS, (lambda t_, p_: (lambda _nc: _nc.gpsimd.dma_start(
            out=x2_hbm[128 * t_:128 * (t_ + 1), :], in_=a2(X2T[p_], 2048))))(t, par),
            waits=[("vc", vno)], incs=[("gd", 16)])
        g = P.op(GPS, (lambda t_, p_: (lambda _nc: _nc.gpsimd.dma_start(
            out=xn2_hbm[128 * t_:128 * (t_ + 1), :], in_=a1(XN2T[p_], 2048))))(t, par),
            incs=[("gd", 16)])["gd"]
        xn2_gd[t] = g
        x2_gd[t] = g
        x2_free[par] = g

    p3_mm = P.cur("mm")
    p3_vc = P.cur("vc")
    p3_gd = P.cur("gd")

    for oc in range(4):
        P.op(GPS, (lambda o_: (lambda _nc: _nc.gpsimd.dma_start(
            out=a1(BWT, 32768).rearrange("p (c o) -> p c o", o=2048)[:, :, 512 * o_:512 * (o_ + 1)],
            in_=bwT.rearrange("(c p) o -> p c o", p=128)[:, :, 512 * o_:512 * (o_ + 1)])))(oc),
            waits=[("mm", p3_mm), ("vc", p3_vc)] if oc == 0 else [], incs=[("gd", 16)])
    bwt_gd = [P.cur("gd") - 48 + 16 * k for k in range(4)]
    P.op(GPS, lambda _nc: _nc.gpsimd.dma_start(
        out=a1(AALL, 2048).rearrange("p (c k) -> p c k", k=128),
        in_=aall.rearrange("(c p) k -> p c k", p=128)), incs=[("gd", 16)])
    P.op(GPS, lambda _nc: _nc.gpsimd.dma_start(
        out=a1(BALL, 2048), in_=ball[:, :]), incs=[("gd", 16)])
    P.op(GPS, lambda _nc: _nc.gpsimd.dma_start(
        out=a1(RWT, 128).rearrange("p (c k) -> p c k", k=8),
        in_=rwT.rearrange("(c p) k -> p c k", p=128)), incs=[("gd", 16)])
    ffnw_gd = P.cur("gd")

    rwt_view = a1(RWT, 128).rearrange("p (c k) -> p c k", k=8)
    aall_view = a1(AALL, 2048).rearrange("p (c k) -> p c k", k=128)
    bwt_view = a1(BWT, 32768).rearrange("p (c o) -> p c o", o=2048)

    xn2d_users = [0, 0]
    prev_midg_vc = 0
    prev_midg_gd = 0
    prev_lora_mm = 0
    evict_vc = {}
    out_gd = [0] * 16
    x2r_free_vc = [0, 0]

    for fb in range(4):
        bpar = fb % 2
        wts = [("gd", max(xn2_gd[4 * fb:4 * fb + 4])), ("mm", p3_mm), ("vc", p3_vc)]
        if xn2d_users[bpar]:
            wts.append(("mm", xn2d_users[bpar]))
        for c in range(16):
            P.op(SYN, (lambda f_, p_, c_: (lambda _nc: _nc.sync.dma_start_transpose(
                a1(XN2D[p_] + 512 * c_, 512),
                xn2_hbm[512 * f_:512 * (f_ + 1), 128 * c_:128 * (c_ + 1)])))(fb, bpar, c),
                waits=wts if c == 0 else [], incs=[("ld", 16)])
        xn2d_ldv = P.cur("ld")
        xv = a1(XN2D[bpar], 8192).rearrange("p (c k) -> p c k", k=512)
        for tt in range(4):
            t = 4 * fb + tt
            tpar = t % 2
            pm_vc = prev_midg_vc
            rcol = 2048 if tpar == 0 else 0
            mcol = 2560 if tpar == 0 else 512
            bbank = (0, 1, 2, 3) if tpar == 0 else (4, 5, 6, 7)
            wts = [("gd", x2_gd[t])]
            if t in (0, 1):
                wts.append(("vc", p3_vc))
            elif x2r_free_vc[tpar]:
                wts.append(("vc", x2r_free_vc[tpar]))
            x2r_ld = P.op(SYN, (lambda t_, p_: (lambda _nc: _nc.sync.dma_start(
                out=a2(XT[p_], 2048), in_=x2_hbm[128 * t_:128 * (t_ + 1), :])))(t, tpar),
                waits=wts, incs=[("ld", 16)])["ld"]
            wts = [("ld", xn2d_ldv), ("gd", ffnw_gd)]
            if (t - 1, 0) in evict_vc:
                wts.append(("vc", evict_vc[(t - 1, 0)]))
            for c in range(16):
                P.op(TEN, (lambda c_, t_, xvv, rc: (lambda _nc: _nc.tensor.matmul(
                    PS[:, rc:rc + 8], xvv[:, c_, 128 * t_:128 * (t_ + 1)],
                    rwt_view[:, c_, :],
                    start=(c_ == 0), stop=(c_ == 15))))(c, tt, xv, rcol),
                    waits=wts if c == 0 else [], incs=[("mm", 1)] if c == 15 else [])
            mm_r = P.cur("mm")
            wts = []
            if (t - 1, 1) in evict_vc:
                wts.append(("vc", evict_vc[(t - 1, 1)]))
            for c in range(16):
                P.op(TEN, (lambda c_, t_, xvv, mc: (lambda _nc: _nc.tensor.matmul(
                    PS[:, mc:mc + 128], xvv[:, c_, 128 * t_:128 * (t_ + 1)],
                    aall_view[:, c_, :],
                    start=(c_ == 0), stop=(c_ == 15))))(c, tt, xv, mcol),
                    waits=wts if c == 0 else [], incs=[("mm", 1)] if c == 15 else [])
            mm_m = P.cur("mm")
            v = P.op(VEC, (lambda rc: (lambda _nc: _nc.vector.tensor_add(
                glog[:, :], PS[:, rc:rc + 8], rb_sb[:, :])))(rcol),
                waits=[("mm", mm_r), ("ld", ld_p0)], incs=[("vc", 1)])["vc"]
            v = P.op(VEC, lambda _nc: _nc.vector.reduce_max(
                gnm[:, :], glog[:, :], AX.X, negate=True),
                waits=[("vc", v)], incs=[("vc", 1)])["vc"]
            a = P.op(SCA, lambda _nc: _nc.scalar.activation(
                gu[:, :], glog[:, :], AF.Exp, bias=gnm[:, 0:1]),
                waits=[("vc", v)], incs=[("ac", 1)])["ac"]
            v = P.op(VEC, lambda _nc: _nc.vector.reduce_max(
                gm1[:, :], gu[:, :], AX.X), waits=[("ac", a)], incs=[("vc", 1)])["vc"]
            v = P.op(VEC, lambda _nc: _nc.vector.tensor_scalar(
                gmk1[:, :], gu[:, :], gm1[:, 0:1], None, ALU.is_ge),
                waits=[("vc", v)], incs=[("vc", 1)])["vc"]
            v = P.op(VEC, lambda _nc: _nc.vector.scalar_tensor_tensor(
                gu2[:, :], gmk1[:, :], -1e30, gu[:, :], ALU.mult, ALU.add),
                waits=[("vc", v)], incs=[("vc", 1)])["vc"]
            v = P.op(VEC, lambda _nc: _nc.vector.reduce_max(
                gm2[:, :], gu2[:, :], AX.X), waits=[("vc", v)], incs=[("vc", 1)])["vc"]
            v = P.op(VEC, lambda _nc: _nc.vector.tensor_scalar(
                gmk2[:, :], gu[:, :], gm2[:, 0:1], None, ALU.is_ge),
                waits=[("vc", v)], incs=[("vc", 1)])["vc"]
            v = P.op(VEC, lambda _nc: _nc.vector.tensor_mul(
                ggu[:, :], gu[:, :], gmk2[:, :]), waits=[("vc", v)], incs=[("vc", 1)])["vc"]
            v = P.op(VEC, lambda _nc: _nc.vector.reduce_sum(
                gsum[:, :], ggu[:, :], AX.X), waits=[("vc", v)], incs=[("vc", 1)])["vc"]
            v = P.op(VEC, lambda _nc: _nc.vector.reciprocal(
                grec[:, :], gsum[:, :]), waits=[("vc", v)], incs=[("vc", 1)])["vc"]
            v = P.op(VEC, lambda _nc: _nc.vector.tensor_scalar(
                gate[:, :], ggu[:, :], grec[:, 0:1], LORA_SCALING, ALU.mult, ALU.mult),
                waits=[("vc", v)], incs=[("vc", 1)])["vc"]
            wts = [("vc", v), ("mm", mm_m)]
            if prev_midg_gd:
                wts.append(("gd", prev_midg_gd))
            v = P.op(VEC, (lambda mc: (lambda _nc: _nc.vector.tensor_tensor(
                a1(MIDG, 128).rearrange("p (e r) -> p e r", r=RL),
                PS[:, mc:mc + 128].rearrange("p (e r) -> p e r", r=RL),
                gate.ap().unsqueeze(2).broadcast_to([128, E, RL]), ALU.mult)))(mcol),
                waits=wts, incs=[("vc", 1)])["vc"]
            prev_midg_vc = v
            g = P.op(GPS, (lambda t_: (lambda _nc: _nc.gpsimd.dma_start(
                out=midg_hbm[128 * t_:128 * (t_ + 1), :], in_=a1(MIDG, 128))))(t),
                waits=[("vc", v)], incs=[("gd", 16)])["gd"]
            prev_midg_gd = g
            wts = [("gd", g)]
            if prev_lora_mm:
                wts.append(("mm", prev_lora_mm))
            mgld = P.op(SYN, (lambda t_: (lambda _nc: _nc.sync.dma_start_transpose(
                a1(MIDGD, 128), midg_hbm[128 * t_:128 * (t_ + 1), :])))(t),
                waits=wts, incs=[("ld", 16)])["ld"]
            oc_mm = [0] * 4
            for oc in range(4):
                wts = [("gd", bwt_gd[oc])]
                if (t - 2, oc) in evict_vc:
                    wts.append(("vc", evict_vc[(t - 2, oc)]))
                if oc == 0 and pm_vc:
                    wts.append(("vc", pm_vc))
                for c in range(16):
                    P.op(TEN, (lambda c_, o_, t_, xvv, bb_: (lambda _nc: _nc.tensor.matmul(
                        bank(bb_), xvv[:, c_, 128 * t_:128 * (t_ + 1)],
                        bwt_view[:, c_, 512 * o_:512 * (o_ + 1)],
                        start=(c_ == 0), stop=False)))(c, oc, tt, xv, bbank[oc]),
                        waits=wts if c == 0 else [])
                P.op(TEN, (lambda o_, bb_: (lambda _nc: _nc.tensor.matmul(
                    bank(bb_), a1(MIDGD, 128),
                    a1(BALL, 2048)[:, 512 * o_:512 * (o_ + 1)],
                    start=False, stop=True)))(oc, bbank[oc]),
                    waits=[("ld", mgld)] if oc == 0 else [], incs=[("mm", 1)])
                oc_mm[oc] = P.cur("mm")
            prev_lora_mm = oc_mm[3]
            for oc in range(4):
                wts = [("mm", oc_mm[oc])]
                if oc == 0:
                    wts.append(("ld", x2r_ld))
                    if t >= 2 and out_gd[t - 2]:
                        wts.append(("gd", out_gd[t - 2]))
                    if t in (0, 1):
                        wts.append(("gd", p3_gd))
                P.op(VEC, (lambda o_, p_, bb_: (lambda _nc: _nc.vector.tensor_add(
                    a2(X2T[p_], 2048)[:, 512 * o_:512 * (o_ + 1)], bank(bb_),
                    bb_sb[:, 512 * o_:512 * (o_ + 1)])))(oc, tpar, bbank[oc]),
                    waits=wts, incs=[("vc", 1)])
                vv = P.cur("vc")
                P.op(VEC, (lambda o_, p_: (lambda _nc: _nc.vector.tensor_add(
                    a2(X2T[p_], 2048)[:, 512 * o_:512 * (o_ + 1)],
                    a2(X2T[p_], 2048)[:, 512 * o_:512 * (o_ + 1)],
                    a2(XT[p_], 2048)[:, 512 * o_:512 * (o_ + 1)])))(oc, tpar),
                    waits=[("vc", vv)], incs=[("vc", 1)])
                evict_vc[(t, oc)] = P.cur("vc")
            vout = P.cur("vc")
            x2r_free_vc[tpar] = vout
            out_gd[t] = P.op(GPS, (lambda t_, p_: (lambda _nc: _nc.gpsimd.dma_start(
                out=out[128 * t_:128 * (t_ + 1), :], in_=a2(X2T[p_], 2048))))(t, tpar),
                waits=[("vc", vout)], incs=[("gd", 16)])["gd"]
        xn2d_users[bpar] = P.cur("mm")

    final_gd = P.cur("gd")
    P.op(GPS, lambda _nc: _nc.gpsimd.engine_nop(), waits=[("gd", final_gd)])

    ctx = {}
    with nc.Block() as block:
        def emit_stream(ename):
            eng = {TEN: nc.tensor, VEC: nc.vector, SCA: nc.scalar,
                   SYN: nc.sync, GPS: nc.gpsimd}[ename]
            for fn, waits, incs in P.streams[ename]:
                for s, v in waits:
                    if v > 0:
                        eng.wait_ge(sem[s], v)
                if isinstance(fn, tuple) and fn[0] == "ctx":
                    inst = fn[1](nc, ctx)
                else:
                    inst = fn(nc)
                for s, n in incs:
                    assert inst is not None
                    inst.then_inc(sem[s], n)

        @block.tensor
        def _(_):
            emit_stream(TEN)

        @block.vector
        def _(_):
            emit_stream(VEC)

        @block.scalar
        def _(_):
            emit_stream(SCA)

        @block.sync
        def _(_):
            emit_stream(SYN)

        @block.gpsimd
        def _(_):
            emit_stream(GPS)

    es.close()
    return nc


def _rope_tables():
    inv = np.exp(-math.log(10000.0) * np.arange(0, DK, 2, dtype=np.float64) / DK)
    pos = np.arange(S, dtype=np.float64)
    fr = pos[None, :] * inv[:, None]
    cos = np.cos(fr).astype(np.float32)
    sin = np.sin(fr).astype(np.float32)
    cos_d = np.repeat(cos, 2, axis=0)
    sin_star = np.empty((DK, S), np.float32)
    sin_star[0::2] = -sin
    sin_star[1::2] = sin
    return cos_d, sin_star


_NC_CACHE = []


def kernel(L0_attn_x, L0_attn_Wq, L0_attn_Wk, L0_attn_Wv, L0_attn_Wo,
           L0_ffn_base_w, L0_ffn_base_b, L0_ffn_router_w, L0_ffn_router_b,
           L0_ffn_lora_A, L0_ffn_lora_B):
    import ml_dtypes
    bf = ml_dtypes.bfloat16
    x = np.ascontiguousarray(np.asarray(L0_attn_x, dtype=np.float32))
    WqT = np.ascontiguousarray(np.asarray(L0_attn_Wq, np.float32).T)
    WkT = np.ascontiguousarray(np.asarray(L0_attn_Wk, np.float32).T)
    WvT = np.ascontiguousarray(np.asarray(L0_attn_Wv, np.float32).T)
    WoT = np.ascontiguousarray(np.asarray(L0_attn_Wo, np.float32).T)
    bwT = np.ascontiguousarray(np.asarray(L0_ffn_base_w, np.float32).T)
    bb = np.ascontiguousarray(np.broadcast_to(
        np.asarray(L0_ffn_base_b, np.float32)[None, :], (128, D)))
    rwT = np.ascontiguousarray(np.asarray(L0_ffn_router_w, np.float32).T)
    rb = np.ascontiguousarray(np.broadcast_to(
        np.asarray(L0_ffn_router_b, np.float32)[None, :], (128, E)))
    aall = np.ascontiguousarray(
        np.asarray(L0_ffn_lora_A, np.float32).transpose(1, 0, 2).reshape(D, E * RL))
    ball = np.ascontiguousarray(
        np.asarray(L0_ffn_lora_B, np.float32).reshape(E * RL, D))

    cos_d, sin_star = _rope_tables()
    ropec = np.ascontiguousarray(cos_d.astype(bf))
    ropes = np.ascontiguousarray(sin_star.astype(bf))
    kk = np.arange(128)[:, None]
    qq = np.arange(QBS)[None, :]
    dm = np.concatenate([(128 * jj + kk <= qq) for jj in range(8)],
                        axis=1).astype(np.float32)
    dmask = np.ascontiguousarray(dm.astype(bf))
    psw = np.zeros((128, 128), np.float32)
    idx = np.arange(128)
    psw[idx, idx ^ 1] = 1.0
    psw = np.ascontiguousarray(psw.astype(bf))

    if not _NC_CACHE:
        _NC_CACHE.append(build_nc())
    nc = _NC_CACHE[0]
    in_maps = []
    for c in range(8):
        b_, r_ = c // 4, c % 4
        in_maps.append({
            "x": x[b_],
            "xq": np.ascontiguousarray(x[b_][QT * r_:QT * (r_ + 1)]),
            "wqT": np.ascontiguousarray(WqT[:, 512 * r_:512 * (r_ + 1)]),
            "wkT": np.ascontiguousarray(WkT[:, 512 * r_:512 * (r_ + 1)]),
            "wvT": np.ascontiguousarray(WvT[:, 512 * r_:512 * (r_ + 1)]),
            "woT": WoT, "bwT": bwT, "bb": bb, "rwT": rwT, "rb": rb,
            "aall": aall, "ball": ball, "ropec": ropec, "ropes": ropes,
            "dmask": dmask, "pswap": psw,
            "rowoff": np.array([[QT * r_]], dtype=np.int32),
        })
    res = run_bass_kernel_spmd(nc, in_maps, core_ids=list(range(8)))
    outp = np.empty((B, S, D), np.float32)
    for c in range(8):
        b_, r_ = c // 4, c % 4
        outp[b_, QT * r_:QT * (r_ + 1), :] = res.results[c]["out"]
    return outp


# version 14
# speedup vs baseline: 1.1036x; 1.1036x over previous
import math
import contextlib
import numpy as np

import concourse.bass as bass
import concourse.mybir as mybir
from concourse.bass_utils import run_bass_kernel_spmd

F32 = mybir.dt.float32
BF16 = mybir.dt.bfloat16
I32 = mybir.dt.int32
AF = mybir.ActivationFunctionType
ALU = mybir.AluOpType
AX = mybir.AxisListType

B, S, D = 2, 8192, 2048
NH, DK = 16, 128
HL = 4
QT = 2048
NT = S
NTT = NT // 128
NFB = NT // 512
QBS = 1024
NQB = NT // QBS
E, RL = 8, 16
SCALE = 1.0 / math.sqrt(DK)
LORA_SCALING = 2.0

TEN, VEC, SCA, SYN, GPS = "tensor", "vector", "scalar", "sync", "gpsimd"


class Plan:
    def __init__(self):
        self.streams = {e: [] for e in (TEN, VEC, SCA, SYN, GPS)}
        self.ct = {}

    def op(self, eng, fn, waits=(), incs=()):
        for s, v in waits:
            have = self.ct.get(s, 0)
            assert v <= have, f"wait {s}>={v} but only {have} planned ({eng})"
        after = {}
        for s, n in incs:
            self.ct[s] = self.ct.get(s, 0) + n
            after[s] = self.ct[s]
        self.streams[eng].append((fn, tuple(waits), tuple(incs)))
        return after

    def cur(self, s):
        return self.ct.get(s, 0)


def build_nc(debug=False):
    nc = bass.Bass("TRN2", target_bir_lowering=False)

    x_in = nc.dram_tensor("x", [NT, D], F32, kind="ExternalInput").ap()
    xq_in = nc.dram_tensor("xq", [QT, D], F32, kind="ExternalInput").ap()
    wqT = nc.dram_tensor("wqT", [D, HL * DK], F32, kind="ExternalInput").ap()
    wkT = nc.dram_tensor("wkT", [D, HL * DK], F32, kind="ExternalInput").ap()
    wvT = nc.dram_tensor("wvT", [D, HL * DK], F32, kind="ExternalInput").ap()
    woT = nc.dram_tensor("woT", [D, D], F32, kind="ExternalInput").ap()
    bwT = nc.dram_tensor("bwT", [D, D], F32, kind="ExternalInput").ap()
    bb_in = nc.dram_tensor("bb", [128, D], F32, kind="ExternalInput").ap()
    rwT = nc.dram_tensor("rwT", [D, E], F32, kind="ExternalInput").ap()
    rb_in = nc.dram_tensor("rb", [128, E], F32, kind="ExternalInput").ap()
    aall = nc.dram_tensor("aall", [D, E * RL], F32, kind="ExternalInput").ap()
    ball = nc.dram_tensor("ball", [E * RL, D], F32, kind="ExternalInput").ap()
    ropec = nc.dram_tensor("ropec", [DK, NT], BF16, kind="ExternalInput").ap()
    ropes = nc.dram_tensor("ropes", [DK, NT], BF16, kind="ExternalInput").ap()
    dmask_in = nc.dram_tensor("dmask", [128, 8 * QBS], BF16, kind="ExternalInput").ap()
    pswap_in = nc.dram_tensor("pswap", [128, 128], BF16, kind="ExternalInput").ap()
    rowoff_in = nc.dram_tensor("rowoff", [1, 1], I32, kind="ExternalInput").ap()
    g3off_in = nc.dram_tensor("g3off", [1, 4], I32, kind="ExternalInput").ap()
    out = nc.dram_tensor("out", [QT, D], F32, kind="ExternalOutput").ap()

    xn_hbm = nc.dram_tensor("xn_hbm", [NT, D], BF16, kind="Internal").ap()
    q_spill = nc.dram_tensor("q_spill", [HL * DK, NT], BF16, kind="Internal").ap()
    k_spill = nc.dram_tensor("k_spill", [HL * DK, NT], BF16, kind="Internal").ap()
    v_spill = nc.dram_tensor("v_spill", [HL * NT, DK], BF16, kind="Internal").ap()
    ag_in_h = [nc.dram_tensor(f"ag_in{h}", [NT, DK], BF16, kind="Internal").ap()
               for h in range(HL)]
    ag_out_h = [nc.dram_tensor(f"ag_out{h}", [4 * NT, DK], BF16, kind="Internal").ap()
                for h in range(HL)]
    ctx_mine = nc.dram_tensor("ctx_mine", [4 * QT, HL * DK], BF16, kind="Internal").ap()
    x2_hbm = nc.dram_tensor("x2_hbm", [QT, D], F32, kind="Internal").ap()
    xn2_hbm = nc.dram_tensor("xn2_hbm", [QT, D], BF16, kind="Internal").ap()
    midg_hbm = nc.dram_tensor("midg_hbm", [QT, E * RL], BF16, kind="Internal").ap()

    P = Plan()
    es = contextlib.ExitStack()

    AR1 = es.enter_context(nc.sbuf_tensor("AR1", [128, 75776], BF16))
    AR2 = es.enter_context(nc.sbuf_tensor("AR2", [128, 8192], F32))

    def a1(off, n):
        return AR1[:, off:off + n]

    def a2(off, n):
        return AR2[:, off:off + n]

    bb_sb = es.enter_context(nc.sbuf_tensor("bb_sb", [128, D], F32))
    rb_sb = es.enter_context(nc.sbuf_tensor("rb_sb", [128, E], F32))
    pswap = es.enter_context(nc.sbuf_tensor("pswap_sb", [128, 128], BF16))
    eps_sb = es.enter_context(nc.sbuf_tensor("eps_sb", [128, 1], F32))
    stats = es.enter_context(nc.sbuf_tensor("stats", [128, 2 * 24], F32))
    mv = es.enter_context(nc.sbuf_tensor("mv", [128, 2 * 2], F32))
    lnt = es.enter_context(nc.sbuf_tensor("lnt", [128, 2], F32))
    rstd = es.enter_context(nc.sbuf_tensor("rstd", [128, 2], F32))
    mv8 = es.enter_context(nc.sbuf_tensor("mv8", [128, 2 * 16], F32))
    lnt8 = es.enter_context(nc.sbuf_tensor("lnt8", [128, 2 * 8], F32))
    rstd8 = es.enter_context(nc.sbuf_tensor("rstd8", [128, 2 * 8], F32))
    bias8 = es.enter_context(nc.sbuf_tensor("bias8", [128, 2 * 8], F32))
    recip = es.enter_context(nc.sbuf_tensor("recip", [128, 8], F32))
    glog = es.enter_context(nc.sbuf_tensor("glog", [128, E], F32))
    gnm = es.enter_context(nc.sbuf_tensor("gnm", [128, 1], F32))
    gu = es.enter_context(nc.sbuf_tensor("gu", [128, E], F32))
    gm1 = es.enter_context(nc.sbuf_tensor("gm1", [128, 1], F32))
    gmk1 = es.enter_context(nc.sbuf_tensor("gmk1", [128, E], F32))
    gu2 = es.enter_context(nc.sbuf_tensor("gu2", [128, E], F32))
    gm2 = es.enter_context(nc.sbuf_tensor("gm2", [128, 1], F32))
    gmk2 = es.enter_context(nc.sbuf_tensor("gmk2", [128, E], F32))
    ggu = es.enter_context(nc.sbuf_tensor("ggu", [128, E], F32))
    gsum = es.enter_context(nc.sbuf_tensor("gsum", [128, 1], F32))
    grec = es.enter_context(nc.sbuf_tensor("grec", [128, 1], F32))
    gate = es.enter_context(nc.sbuf_tensor("gate", [128, E], F32))
    roff_sb = es.enter_context(nc.sbuf_tensor("roff_sb", [1, 1], I32))
    g3off_sb = es.enter_context(nc.sbuf_tensor("g3off_sb", [1, 4], I32))
    dmask = es.enter_context(nc.sbuf_tensor("dmask_sb", [128, 8 * QBS], BF16))

    PS = es.enter_context(nc.psum_tensor("PS", [128, 4096], F32))

    def bank(i, n=512):
        return PS[:, 512 * i: 512 * i + n]

    sem = {}
    for name in ("ld", "gd", "mm", "vc", "ac", "cc"):
        sem[name] = es.enter_context(nc.semaphore(name))

    WQ, WK, WV = 0, 8192, 16384
    XND = (24576, 32768, 40960, 49152)
    XNT = (57344, 59392)
    QEV = (61440, 62464)
    QSW = (63488, 64512)
    RT1, RT2 = 65536, 66560
    QR = (67584, 68608)
    VST = (69632, 70144)
    RPC = (70656, 71680)
    RPS = (72704, 73728)
    KBF = (0, 8192)
    VBF = (16384, 24832)
    QBF = (33280, 34304)
    PTS = (35328, 37376)
    CTXS = (39424, 40448)
    WOT = 43008
    CTXD = 0
    XN2T = (32768, 34816)
    BWT = 43008
    XN2D = (0, 8192)
    AALL = 16384
    BALL = 18432
    RWT = 20480
    MIDG = 20608
    MIDGD = 20736
    XT = (0, 2048)
    X2T = (4096, 6144)

    for dst, src in ((pswap[:, :], pswap_in[:, :]), (bb_sb[:, :], bb_in[:, :]),
                     (rb_sb[:, :], rb_in[:, :]), (dmask[:, :], dmask_in[:, :])):
        P.op(SYN, (lambda d, s: (lambda _nc: _nc.sync.dma_start(out=d, in_=s)))(dst, src),
             incs=[("ld", 16)])
    for off, src in ((WQ, wqT), (WK, wkT), (WV, wvT)):
        dst = a1(off, 8192).rearrange("p (c k) -> p c k", k=512)
        srcr = src.rearrange("(c p) k -> p c k", p=128)
        P.op(GPS, (lambda d, s: (lambda _nc: _nc.gpsimd.dma_start(out=d, in_=s)))(dst, srcr),
             incs=[("gd", 16)])
    P.op(GPS, lambda _nc: _nc.gpsimd.dma_start(out=roff_sb[:, :], in_=rowoff_in[:, :]),
         incs=[("gd", 16)])
    P.op(GPS, lambda _nc: _nc.gpsimd.dma_start(out=g3off_sb[:, :], in_=g3off_in[:, :]),
         incs=[("gd", 16)])
    ld_p0 = P.cur("ld")
    gd_p0 = P.cur("gd")
    P.op(VEC, lambda _nc: _nc.vector.memset(eps_sb[:, :], 1e-5), incs=[("vc", 1)])

    def _mk_rowoff(_nc, _ctx):
        reg = _nc.gpsimd.alloc_register("rowoff_reg")
        _nc.gpsimd.reg_load(reg, roff_sb[0:1, 0:1])
        _ctx["rowoff"] = _nc.gpsimd.snap(reg, min_val=0, max_val=3 * QT)
        _ctx["g3off"] = []
        for p_ in range(4):
            rg = _nc.gpsimd.alloc_register(f"g3off_reg{p_}")
            _nc.gpsimd.reg_load(rg, g3off_sb[0:1, p_:p_ + 1])
            _ctx["g3off"].append(_nc.gpsimd.snap(rg, min_val=0, max_val=4 * NT - QT))
    P.op(GPS, ("ctx", _mk_rowoff), waits=[("gd", gd_p0)])

    NDB = NFB // 2
    xn_wr_gd = [0] * NTT
    xt_free = [0, 0, 0, 0]
    xn_free = [0, 0]

    def make_ln_steps(db):
        dpar = db % 2
        mv8v = mv8.ap().rearrange("p (d k two) -> p d two k", d=2, two=2)
        ld_tile = [0] * 8

        def emit_x_load(i8):
            i = 8 * db + i8
            par = i % 4
            w = [("vc", xt_free[par])] if xt_free[par] else []
            ld_tile[i8] = P.op(SYN, (lambda i_, p_: (lambda _nc: _nc.sync.dma_start(
                out=a2(2048 * p_, 2048), in_=x_in[128 * i_:128 * (i_ + 1), :])))(i, par),
                waits=w, incs=[("ld", 16)])["ld"]

        for i8 in range(4):
            emit_x_load(i8)

        def step(pr):
            for j in range(2):
                i8 = 2 * pr + j
                par = (8 * db + i8) % 4
                spar = (8 * db + i8) % 2
                for c in range(4):
                    P.op(VEC, (lambda p_, sp_, c_: (lambda _nc: _nc.vector.bn_stats(
                        stats[:, 24 * sp_ + 6 * c_:24 * sp_ + 6 * (c_ + 1)],
                        a2(2048 * p_ + 512 * c_, 512))))(par, spar, c),
                        waits=[("ld", ld_tile[i8])] if c == 0 else [],
                        incs=[("vc", 1)] if c == 3 else [])
                vbn = P.cur("vc")
                P.op(VEC, (lambda sp_, d_, k_: (lambda _nc: _nc.vector.bn_aggr(
                    mv8[:, 16 * d_ + 2 * k_:16 * d_ + 2 * k_ + 2],
                    stats[:, 24 * sp_:24 * sp_ + 24])))(spar, dpar, i8),
                    waits=[("vc", vbn)], incs=[("vc", 1)])
            vagg = P.cur("vc")
            aln = P.op(SCA, (lambda d_, r_, vv: (lambda _nc: _nc.scalar.activation(
                lnt8[:, 8 * d_ + 2 * r_:8 * d_ + 2 * r_ + 2],
                vv[:, d_, 1, 2 * r_:2 * r_ + 2], AF.Ln,
                bias=eps_sb[:, 0:1])))(dpar, pr, mv8v), waits=[("vc", vagg)],
                incs=[("ac", 1)])["ac"]
            aex = P.op(SCA, (lambda d_, r_: (lambda _nc: _nc.scalar.activation(
                rstd8[:, 8 * d_ + 2 * r_:8 * d_ + 2 * r_ + 2],
                lnt8[:, 8 * d_ + 2 * r_:8 * d_ + 2 * r_ + 2], AF.Exp,
                scale=-0.5)))(dpar, pr), waits=[("ac", aln)], incs=[("ac", 1)])["ac"]
            for j in range(2):
                i8 = 2 * pr + j
                i = 8 * db + i8
                par = i % 4
                spar = i % 2
                wts = [("ac", aex)]
                if xn_free[spar]:
                    wts.append(("gd", xn_free[spar]))
                vno = P.op(VEC, (lambda p_, sp_, d_, k_, vv: (lambda _nc:
                    _nc.vector.tensor_scalar(
                        a1(XNT[sp_], 2048), a2(2048 * p_, 2048),
                        vv[:, d_, 0, k_:k_ + 1],
                        rstd8[:, 8 * d_ + k_:8 * d_ + k_ + 1],
                        ALU.subtract, ALU.mult)))(par, spar, dpar, i8, mv8v),
                    waits=wts, incs=[("vc", 1)])["vc"]
                xt_free[par] = vno
                g = P.op(GPS, (lambda i_, sp_: (lambda _nc: _nc.gpsimd.dma_start(
                    out=xn_hbm[128 * i_:128 * (i_ + 1), :],
                    in_=a1(XNT[sp_], 2048))))(i, spar),
                    waits=[("vc", vno)], incs=[("gd", 16)])["gd"]
                xn_wr_gd[i] = g
                xn_free[spar] = g
            for j in range(2):
                if 2 * pr + 4 + j < 8:
                    emit_x_load(2 * pr + 4 + j)

        return [lambda pr=pr: step(pr) for pr in range(4)]

    xnd_rd_ld = [0] * NDB
    xnd_users_mm = [0, 0]
    rope_tab_ld = [0, 0]
    psq_free_ac = [0, 0]
    psw_free_ac = 0
    psv_free_ac = [0, 0]
    qev_free_mm = [0, 0]
    qev_free_vc = [0, 0]
    qsw_free_vc = [0, 0]
    qr_free_gd = [0, 0]
    vst_free_gd = [0, 0]
    rope_tab_users_vc = [0, 0]

    def emit_block_reads(db):
        par = db % 2
        w = [("gd", max(xn_wr_gd[8 * db + k] for k in range(8)))]
        if xnd_users_mm[par]:
            w.append(("mm", xnd_users_mm[par]))
        base = XND[2 * par]
        last = 0
        for c in range(16):
            last = P.op(SYN, (lambda d_, b_, c_: (lambda _nc: _nc.sync.dma_start_transpose(
                a1(b_ + 1024 * c_, 1024),
                xn_hbm[1024 * d_:1024 * (d_ + 1), 128 * c_:128 * (c_ + 1)])))(db, base, c),
                waits=w if c == 0 else [], incs=[("ld", 16)])["ld"]
        xnd_rd_ld[db] = last

    def emit_rope_table_loads(db):
        dpar = db % 2
        wts = []
        if rope_tab_users_vc[dpar]:
            wts.append(("gd", rope_tab_users_vc[dpar]))
        P.op(SYN, (lambda d_, p_: (lambda _nc: _nc.sync.dma_start(
            out=a1(RPC[p_], 1024), in_=ropec[:, QBS * d_:QBS * (d_ + 1)])))(db, dpar),
            waits=wts, incs=[("ld", 16)])
        rope_tab_ld[dpar] = P.op(SYN, (lambda d_, p_: (lambda _nc: _nc.sync.dma_start(
            out=a1(RPS[p_], 1024), in_=ropes[:, QBS * d_:QBS * (d_ + 1)])))(db, dpar),
            incs=[("ld", 16)])["ld"]

    def emit_proj_db(db, ln_steps):
        nonlocal psw_free_ac
        dpar = db % 2
        base = XND[2 * dpar]
        R = a1(base, 16384).rearrange("p (c b k) -> p c b k", b=2, k=512)
        rd_ld = xnd_rd_ld[db]
        jobs = [(kind, h) for h in range(HL) for kind in ("q", "k")]

        def emit_mms(ji):
            kind, h = jobs[ji]
            par = ji % 2
            woff = WQ if kind == "q" else WK
            wview = a1(woff, 8192).rearrange("p (c k) -> p c k", k=512)
            wts = [("ld", rd_ld), ("gd", gd_p0)]
            if psq_free_ac[par]:
                wts.append(("ac", psq_free_ac[par]))
            for c in range(16):
                for half in range(2):
                    P.op(TEN, (lambda p_, c_, hf, wv, h_: (lambda _nc: _nc.tensor.matmul(
                        PS[:, 1024 * p_ + 512 * hf:1024 * p_ + 512 * (hf + 1)],
                        wv[:, c_, 128 * h_:128 * (h_ + 1)], R[:, c_, hf, :],
                        start=(c_ == 0), stop=(c_ == 15))))(par, c, half, wview, h),
                        waits=wts if (c == 0 and half == 0) else [],
                        incs=[("mm", 1)] if (c == 15 and half == 1) else [])
            return P.cur("mm")

        def emit_evict(ji, mmv):
            par = ji % 2
            wts = [("mm", mmv)]
            if qev_free_mm[par]:
                wts.append(("mm", qev_free_mm[par]))
            if qev_free_vc[par]:
                wts.append(("gd", qev_free_vc[par]))
            a = P.op(SCA, (lambda p_: (lambda _nc: _nc.scalar.copy(
                a1(QEV[p_], 1024), PS[:, 1024 * p_:1024 * (p_ + 1)])))(par),
                waits=wts, incs=[("ac", 1)])["ac"]
            psq_free_ac[par] = a
            return a

        def emit_swap(ji, acv):
            nonlocal psw_free_ac
            par = ji % 2
            wts = [("ac", acv), ("ld", ld_p0)]
            if psw_free_ac:
                wts.append(("ac", psw_free_ac))
            for half in range(2):
                P.op(TEN, (lambda p_, hf: (lambda _nc: _nc.tensor.matmul(
                    PS[:, 2048 + 512 * hf:2048 + 512 * (hf + 1)], pswap[:, :],
                    a1(QEV[p_], 1024)[:, 512 * hf:512 * (hf + 1)],
                    start=True, stop=True)))(par, half),
                    waits=wts if half == 0 else [],
                    incs=[("mm", 1)] if half == 1 else [])
            mmv = P.cur("mm")
            wts = [("mm", mmv)]
            if qsw_free_vc[par]:
                wts.append(("gd", qsw_free_vc[par]))
            a = P.op(SCA, (lambda p_: (lambda _nc: _nc.scalar.copy(
                a1(QSW[p_], 1024), PS[:, 2048:3072])))(par), waits=wts,
                incs=[("ac", 1)])["ac"]
            psw_free_ac = a
            return a

        def emit_rope(ji, acv):
            kind, h = jobs[ji]
            par = ji % 2
            P.op(GPS, (lambda p_, d_: (lambda _nc: _nc.gpsimd.tensor_mul(
                a1(RT1, 1024), a1(QEV[p_], 1024), a1(RPC[d_], 1024))))(par, dpar),
                waits=[("ac", acv), ("ld", rope_tab_ld[dpar])])
            v2 = P.op(GPS, (lambda p_, d_: (lambda _nc: _nc.gpsimd.tensor_mul(
                a1(RT2, 1024), a1(QSW[p_], 1024), a1(RPS[d_], 1024))))(par, dpar),
                incs=[("gd", 1)])["gd"]
            qev_free_vc[par] = v2
            qsw_free_vc[par] = v2
            wts = [("gd", max(v2, qr_free_gd[par]))]
            v3 = P.op(GPS, (lambda p_: (lambda _nc: _nc.gpsimd.tensor_add(
                a1(QR[p_], 1024), a1(RT1, 1024), a1(RT2, 1024))))(par),
                waits=wts, incs=[("gd", 1)])["gd"]
            rope_tab_users_vc[dpar] = v3
            spill = q_spill if kind == "q" else k_spill
            g = P.op(GPS, (lambda p_, sp, h_, d_: (lambda _nc: _nc.gpsimd.dma_start(
                out=sp[128 * h_:128 * (h_ + 1), QBS * d_:QBS * (d_ + 1)],
                in_=a1(QR[p_], 1024))))(par, spill, h, db), waits=[("gd", v3)],
                incs=[("gd", 16)])["gd"]
            qr_free_gd[par] = g

        pend = []
        for ji in range(len(jobs)):
            m = emit_mms(ji)
            if pend:
                pj, pm = pend.pop()
                a = emit_evict(pj, pm)
                sa = emit_swap(pj, a)
                emit_rope(pj, sa)
            pend.append((ji, m))
            if ji % 2 == 1 and ln_steps[ji // 2] is not None:
                ln_steps[ji // 2]()
        pj, pm = pend.pop()
        a = emit_evict(pj, pm)
        sa = emit_swap(pj, a)
        emit_rope(pj, sa)

        wvview = a1(WV, 8192).rearrange("p (c k) -> p c k", k=512)
        for tt in range(8):
            par = tt % 2
            wts = [("ld", rd_ld), ("gd", gd_p0)]
            if psv_free_ac[par]:
                wts.append(("ac", psv_free_ac[par]))
            for c in range(16):
                P.op(TEN, (lambda p_, c_, t_, wv: (lambda _nc: _nc.tensor.matmul(
                    PS[:, 3072 + 512 * p_:3072 + 512 * (p_ + 1)],
                    R[:, c_, t_ // 4, 128 * (t_ % 4):128 * (t_ % 4 + 1)], wv[:, c_, :],
                    start=(c_ == 0), stop=(c_ == 15))))(par, c, tt, wvview),
                    waits=wts if c == 0 else [], incs=[("mm", 1)] if c == 15 else [])
            mmv = P.cur("mm")
            wts = [("mm", mmv)]
            if vst_free_gd[par]:
                wts.append(("gd", vst_free_gd[par]))
            av = P.op(SCA, (lambda p_: (lambda _nc: _nc.scalar.copy(
                a1(VST[p_], 512), PS[:, 3072 + 512 * p_:3072 + 512 * (p_ + 1)])))(par),
                waits=wts, incs=[("ac", 1)])["ac"]
            psv_free_ac[par] = av
            trow = QBS * db + 128 * tt
            vdst = v_spill.rearrange("(h t) d -> h t d", h=HL)[
                :, trow:trow + 128, :].rearrange("h p d -> p h d")
            vsrc = a1(VST[par], 512).rearrange("p (h d) -> p h d", d=DK)
            g = P.op(GPS, (lambda dd, ss: (lambda _nc: _nc.gpsimd.dma_start(
                out=dd, in_=ss)))(vdst, vsrc), waits=[("ac", av)], incs=[("gd", 16)])["gd"]
            vst_free_gd[par] = g
        xnd_users_mm[dpar] = P.cur("mm")

    for db0 in (0, 1):
        for s in make_ln_steps(db0):
            s()
    emit_rope_table_loads(0)
    for db in range(NDB):
        emit_block_reads(db)
        if db + 1 < NDB:
            emit_rope_table_loads(db + 1)
        steps = make_ln_steps(db + 2) if db + 2 < NDB else [None] * 4
        emit_proj_db(db, steps)

    p1_mm = P.cur("mm")
    p1_vc = P.cur("vc")
    p1_gd = P.cur("gd")

    P.op(GPS, lambda _nc: _nc.gpsimd.dma_start(
        out=a1(WOT, 32768).rearrange("p (c o) -> p c o", o=2048),
        in_=woT.rearrange("(c p) o -> p c o", p=128)),
        waits=[("mm", p1_mm), ("vc", p1_vc)], incs=[("gd", 16)])
    wot_gd = P.cur("gd")

    kv_ld = [0, 0]
    kv_free_mm = [0, 0]
    vones_vc = [0, 0]

    def emit_kv_load(h):
        par = h % 2
        wts = [("gd", p1_gd)]
        if kv_free_mm[par]:
            wts.append(("mm", kv_free_mm[par]))
        P.op(SYN, (lambda p_, h_: (lambda _nc: _nc.sync.dma_start(
            out=a1(KBF[p_], 8192), in_=k_spill[128 * h_:128 * (h_ + 1), :])))(par, h),
            waits=wts, incs=[("ld", 16)])
        vdst = a1(VBF[par], 8448).rearrange("p (n c) -> p n c", c=132)[:, :, 0:128]
        vsrc = v_spill[NT * h:NT * (h + 1), :].rearrange("(n p) d -> p n d", p=128)
        P.op(SYN, (lambda dd, ss: (lambda _nc: _nc.sync.dma_start(out=dd, in_=ss)))(
            vdst, vsrc), incs=[("ld", 16)])
        kv_ld[par] = P.cur("ld")
        vcols = a1(VBF[par], 8448).rearrange("p (n c) -> p n c", c=132)[:, :, 128:129]
        vones_vc[par] = P.op(VEC, (lambda vv: (lambda _nc: _nc.vector.memset(vv, 1.0)))(
            vcols), waits=[("ld", kv_ld[par])], incs=[("vc", 1)])["vc"]

    qld = [0, 0]
    q_free_mm = [0, 0]
    pt_ps_free = [0, 0]
    pt_sb_free = [0, 0]
    ctxs_free_gd = [0, 0]
    ctx_evict_vc = 0
    chunk_ctr = 0

    emit_kv_load(0)

    for h in range(HL):
        hpar = h % 2
        if h + 1 < HL:
            emit_kv_load(h + 1)
        for qb in range(NQB):
            qpar = qb % 2
            wts = [("gd", p1_gd)]
            if q_free_mm[qpar]:
                wts.append(("mm", q_free_mm[qpar]))
            qld[qpar] = P.op(SYN, (lambda p_, h_, q_: (lambda _nc: _nc.sync.dma_start(
                out=a1(QBF[p_], 1024),
                in_=q_spill[128 * h_:128 * (h_ + 1), QBS * q_:QBS * (q_ + 1)])))(
                qpar, h, qb), waits=wts, incs=[("ld", 16)])["ld"]
            nch = 8 * (qb + 1)
            exp_ac = {}
            mask_vc = {}

            def emit_sT(j, nch_, qpar_, hpar_):
                nonlocal chunk_ctr
                par = chunk_ctr % 2
                wts = [("ld", max(qld[qpar_], kv_ld[hpar_])), ("vc", vones_vc[hpar_])]
                if pt_ps_free[par]:
                    wts.append(("ac", pt_ps_free[par]))
                for half in range(2):
                    P.op(TEN, (lambda p_, j_, hf, hp, qp: (lambda _nc: _nc.tensor.matmul(
                        PS[:, 1024 * p_ + 512 * hf: 1024 * p_ + 512 * (hf + 1)],
                        a1(KBF[hp], 8192)[:, 128 * j_:128 * (j_ + 1)],
                        a1(QBF[qp], 1024)[:, 512 * hf:512 * (hf + 1)],
                        start=True, stop=True)))(par, j, half, hpar_, qpar_),
                        waits=wts if half == 0 else [],
                        incs=[("mm", 1)] if half == 1 else [])
                w2 = [("mm", P.cur("mm"))]
                if pt_sb_free[par]:
                    w2.append(("mm", pt_sb_free[par]))
                a = P.op(SCA, (lambda p_: (lambda _nc: _nc.scalar.activation(
                    a1(PTS[p_], 1024), PS[:, 1024 * p_:1024 * (p_ + 1)],
                    AF.Exp, scale=SCALE)))(par), waits=w2, incs=[("ac", 1)])["ac"]
                pt_ps_free[par] = a
                exp_ac[j] = a
                if j >= nch_ - 8:
                    jj = j - (nch_ - 8)
                    v = P.op(VEC, (lambda p_, jj_: (lambda _nc: _nc.vector.tensor_mul(
                        a1(PTS[p_], 1024), a1(PTS[p_], 1024),
                        dmask[:, QBS * jj_:QBS * (jj_ + 1)])))(par, jj),
                        waits=[("ac", a), ("ld", ld_p0)], incs=[("vc", 1)])["vc"]
                    mask_vc[j] = v
                chunk_ctr += 1
                return chunk_ctr - 1

            def emit_ctx(j, gidx, nch_, hpar_):
                par = gidx % 2
                if j in mask_vc:
                    wts = [("vc", mask_vc[j])]
                else:
                    wts = [("ac", exp_ac[j])]
                if j == 0 and ctx_evict_vc:
                    wts.append(("vc", ctx_evict_vc))
                vview = a1(VBF[hpar_], 8448).rearrange("p (n c) -> p n c", c=132)
                for s_ in range(8):
                    P.op(TEN, (lambda p_, j_, ss, vv, n_: (lambda _nc: _nc.tensor.matmul(
                        PS[:, 2048 + 256 * ss: 2048 + 256 * ss + 129],
                        a1(PTS[p_], 1024)[:, 128 * ss:128 * (ss + 1)],
                        vv[:, j_, 0:129],
                        start=(j_ == 0 and ss % 2 == 0),
                        stop=(j_ == n_ - 1))))(par, j, s_, vview, nch_),
                        waits=wts if s_ == 0 else [],
                        incs=[("mm", 1)] if s_ == 7 else [])
                return P.cur("mm")

            prev = None
            for j in range(nch):
                g = emit_sT(j, nch, qpar, hpar)
                if prev is not None:
                    m = emit_ctx(prev[0], prev[1], nch, hpar)
                    pt_sb_free[prev[1] % 2] = m
                prev = (j, g)
            m = emit_ctx(prev[0], prev[1], nch, hpar)
            pt_sb_free[prev[1] % 2] = m
            ctx_done = m
            for s_ in range(8):
                P.op(VEC, (lambda ss: (lambda _nc: _nc.vector.reciprocal(
                    recip[:, ss:ss + 1],
                    PS[:, 2048 + 256 * ss + 128:2048 + 256 * ss + 129])))(s_),
                    waits=[("mm", ctx_done)] if s_ == 0 else [],
                    incs=[("vc", 1)] if s_ == 7 else [])
            vrec = P.cur("vc")
            wts = [("vc", vrec)]
            if ctxs_free_gd[qpar]:
                wts.append(("gd", ctxs_free_gd[qpar]))
            for s_ in range(8):
                P.op(VEC, (lambda p_, ss: (lambda _nc: _nc.vector.tensor_scalar_mul(
                    a1(CTXS[p_], 1024)[:, 128 * ss:128 * (ss + 1)],
                    PS[:, 2048 + 256 * ss:2048 + 256 * ss + 128],
                    recip[:, ss:ss + 1])))(qpar, s_),
                    waits=wts if s_ == 0 else [], incs=[("vc", 1)] if s_ == 7 else [])
            ctx_evict_vc = P.cur("vc")
            agdst = ag_in_h[h][QBS * qb:QBS * (qb + 1), :].rearrange(
                "(s p) d -> p s d", p=128)
            agsrc = a1(CTXS[qpar], 1024).rearrange("p (s d) -> p s d", d=128)
            g2 = P.op(GPS, (lambda dd, ss: (lambda _nc: _nc.gpsimd.dma_start(
                out=dd, in_=ss)))(agdst, agsrc), waits=[("vc", ctx_evict_vc)],
                incs=[("gd", 16)])["gd"]
            ctxs_free_gd[qpar] = g2
            q_free_mm[qpar] = ctx_done
            if h == 3 and qb == 3:
                P.op(GPS, lambda _nc: _nc.gpsimd.collective_compute(
                    "AllGather", ALU.bypass, ins=[ag_in_h[3][0:NT // 2, :].opt()],
                    outs=[ag_out_h[3][0:2 * NT, :].opt()],
                    replica_groups=[[0, 1, 2, 3], [4, 5, 6, 7]]),
                    waits=[("gd", P.cur("gd"))], incs=[("cc", 1)])
        if h < 3:
            P.op(GPS, (lambda h_: (lambda _nc: _nc.gpsimd.collective_compute(
                "AllGather", ALU.bypass, ins=[ag_in_h[h_].opt()],
                outs=[ag_out_h[h_].opt()],
                replica_groups=[[0, 1, 2, 3], [4, 5, 6, 7]])))(h),
                waits=[("gd", P.cur("gd"))], incs=[("cc", 1)])
            cch = P.cur("cc")
            for p_ in range(4):
                def _mk_gather(pp, hh):
                    def f(_nc, _ctx):
                        v = _ctx["rowoff"]
                        return _nc.gpsimd.dma_start(
                            out=ctx_mine[QT * pp:QT * (pp + 1), 128 * hh:128 * (hh + 1)],
                            in_=ag_out_h[hh][NT * pp:NT * (pp + 1), :][bass.ds(v, QT), :])
                    return f
                P.op(GPS, ("ctx", _mk_gather(p_, h)),
                     waits=[("cc", cch)] if p_ == 0 else [], incs=[("gd", 16)])
        else:
            P.op(GPS, lambda _nc: _nc.gpsimd.collective_compute(
                "AllGather", ALU.bypass, ins=[ag_in_h[3][NT // 2:NT, :].opt()],
                outs=[ag_out_h[3][2 * NT:4 * NT, :].opt()],
                replica_groups=[[0, 1, 2, 3], [4, 5, 6, 7]]),
                waits=[("gd", P.cur("gd"))], incs=[("cc", 1)])
            cch = P.cur("cc")
            for p_ in range(4):
                def _mk_gather3(pp):
                    def f(_nc, _ctx):
                        v = _ctx["g3off"][pp]
                        return _nc.gpsimd.dma_start(
                            out=ctx_mine[QT * pp:QT * (pp + 1), 384:512],
                            in_=ag_out_h[3][bass.ds(v, QT), :])
                    return f
                P.op(GPS, ("ctx", _mk_gather3(p_)),
                     waits=[("cc", cch)] if p_ == 0 else [], incs=[("gd", 16)])
        kv_free_mm[hpar] = P.cur("mm")

    p2_gd = P.cur("gd")
    p2_mm = P.cur("mm")
    p2_vc = P.cur("vc")
    gather_gd = P.cur("gd")
    for ci in range(16):
        pq, hl_ = ci // 4, ci % 4
        P.op(SYN, (lambda p_, h_, c_: (lambda _nc: _nc.sync.dma_start_transpose(
            a1(CTXD + 2048 * c_, 2048),
            ctx_mine[QT * p_:QT * (p_ + 1), 128 * h_:128 * (h_ + 1)])))(pq, hl_, ci),
            waits=[("gd", gather_gd), ("mm", p2_mm), ("vc", p2_vc)] if ci == 0 else [],
            incs=[("ld", 16)])
    ctxd_ld = P.cur("ld")

    xq_free_vc = [0, 0]
    x2_free = [0, 0]
    op_ps_free = [0, 0]
    xn2_gd = [0] * 16
    x2_gd = [0] * 16

    wot_view = a1(WOT, 32768).rearrange("p (c o) -> p c o", o=2048)
    ctxd_view = a1(CTXD, 32768).rearrange("p (c t) -> p c t", t=2048)

    for t in range(16):
        par = t % 2
        wts = []
        if xq_free_vc[par]:
            wts.append(("vc", xq_free_vc[par]))
        xq_ld = P.op(SYN, (lambda t_, p_: (lambda _nc: _nc.sync.dma_start(
            out=a2(XT[p_], 2048), in_=xq_in[128 * t_:128 * (t_ + 1), :])))(t, par),
            waits=wts, incs=[("ld", 16)])["ld"]
        for oc in range(4):
            bk = 4 * par + oc
            wts = [("ld", ctxd_ld), ("gd", wot_gd)]
            if op_ps_free[par]:
                wts.append(("vc", op_ps_free[par]))
            for hdc in range(16):
                P.op(TEN, (lambda b_, h_, o_, t_: (lambda _nc: _nc.tensor.matmul(
                    bank(b_), ctxd_view[:, h_, 128 * t_:128 * (t_ + 1)],
                    wot_view[:, h_, 512 * o_:512 * (o_ + 1)],
                    start=(h_ == 0), stop=(h_ == 15))))(bk, hdc, oc, t),
                    waits=wts if (hdc == 0 and oc == 0) else [],
                    incs=[("mm", 1)] if hdc == 15 else [])
        mmv = P.cur("mm")
        wts = [("mm", mmv), ("ld", xq_ld)]
        if x2_free[par]:
            wts.append(("gd", x2_free[par]))
        for oc in range(4):
            bk = 4 * par + oc
            P.op(VEC, (lambda b_, p_, o_: (lambda _nc: _nc.vector.tensor_add(
                a2(X2T[p_], 2048)[:, 512 * o_:512 * (o_ + 1)], bank(b_),
                a2(XT[p_], 2048)[:, 512 * o_:512 * (o_ + 1)])))(bk, par, oc),
                waits=wts if oc == 0 else [], incs=[("vc", 1)] if oc == 3 else [])
        vadd = P.cur("vc")
        op_ps_free[par] = vadd
        xq_free_vc[par] = vadd
        for c in range(4):
            P.op(VEC, (lambda p_, c_: (lambda _nc: _nc.vector.bn_stats(
                stats[:, 24 * p_ + 6 * c_:24 * p_ + 6 * (c_ + 1)],
                a2(X2T[p_] + 512 * c_, 512))))(par, c),
                waits=[("vc", vadd)] if c == 0 else [],
                incs=[("vc", 1)] if c == 3 else [])
        vbn = P.cur("vc")
        vag = P.op(VEC, (lambda p_: (lambda _nc: _nc.vector.bn_aggr(
            mv[:, 2 * p_:2 * p_ + 2], stats[:, 24 * p_:24 * p_ + 24])))(par),
            waits=[("vc", vbn)], incs=[("vc", 1)])["vc"]
        aln = P.op(SCA, (lambda p_: (lambda _nc: _nc.scalar.activation(
            lnt[:, p_:p_ + 1], mv[:, 2 * p_ + 1:2 * p_ + 2], AF.Ln,
            bias=eps_sb[:, 0:1])))(par), waits=[("vc", vag)], incs=[("ac", 1)])["ac"]
        aex = P.op(SCA, (lambda p_: (lambda _nc: _nc.scalar.activation(
            rstd[:, p_:p_ + 1], lnt[:, p_:p_ + 1], AF.Exp, scale=-0.5)))(par),
            waits=[("ac", aln)], incs=[("ac", 1)])["ac"]
        vno = P.op(VEC, (lambda p_: (lambda _nc: _nc.vector.tensor_scalar(
            a1(XN2T[p_], 2048), a2(X2T[p_], 2048), mv[:, 2 * p_:2 * p_ + 1],
            rstd[:, p_:p_ + 1], ALU.subtract, ALU.mult)))(par),
            waits=[("ac", aex)], incs=[("vc", 1)])["vc"]
        P.op(GPS, (lambda t_, p_: (lambda _nc: _nc.gpsimd.dma_start(
            out=x2_hbm[128 * t_:128 * (t_ + 1), :], in_=a2(X2T[p_], 2048))))(t, par),
            waits=[("vc", vno)], incs=[("gd", 16)])
        g = P.op(GPS, (lambda t_, p_: (lambda _nc: _nc.gpsimd.dma_start(
            out=xn2_hbm[128 * t_:128 * (t_ + 1), :], in_=a1(XN2T[p_], 2048))))(t, par),
            incs=[("gd", 16)])["gd"]
        xn2_gd[t] = g
        x2_gd[t] = g
        x2_free[par] = g

    p3_mm = P.cur("mm")
    p3_vc = P.cur("vc")
    p3_gd = P.cur("gd")

    bwt_gd = [0] * 4
    ffnw_gd = 0

    def emit_ffn_weight_loads():
        nonlocal bwt_gd, ffnw_gd
        P.op(GPS, lambda _nc: _nc.gpsimd.dma_start(
            out=a1(RWT, 128).rearrange("p (c k) -> p c k", k=8),
            in_=rwT.rearrange("(c p) k -> p c k", p=128)),
            waits=[("mm", p3_mm), ("vc", p3_vc)], incs=[("gd", 16)])
        P.op(GPS, lambda _nc: _nc.gpsimd.dma_start(
            out=a1(AALL, 2048).rearrange("p (c k) -> p c k", k=128),
            in_=aall.rearrange("(c p) k -> p c k", p=128)), incs=[("gd", 16)])
        P.op(GPS, lambda _nc: _nc.gpsimd.dma_start(
            out=a1(BALL, 2048), in_=ball[:, :]), incs=[("gd", 16)])
        ffnw_gd = P.cur("gd")
        for oc in range(4):
            P.op(GPS, (lambda o_: (lambda _nc: _nc.gpsimd.dma_start(
                out=a1(BWT, 32768).rearrange("p (c o) -> p c o", o=2048)[:, :, 512 * o_:512 * (o_ + 1)],
                in_=bwT.rearrange("(c p) o -> p c o", p=128)[:, :, 512 * o_:512 * (o_ + 1)])))(oc),
                incs=[("gd", 16)])
            bwt_gd[oc] = P.cur("gd")

    rwt_view = a1(RWT, 128).rearrange("p (c k) -> p c k", k=8)
    aall_view = a1(AALL, 2048).rearrange("p (c k) -> p c k", k=128)
    bwt_view = a1(BWT, 32768).rearrange("p (c o) -> p c o", o=2048)

    xn2d_users = [0, 0]
    prev_midg_vc = 0
    prev_midg_gd = 0
    prev_lora_mm = 0
    evict_vc = {}
    out_gd = [0] * 16
    x2r_free_vc = [0, 0]

    for fb in range(4):
        bpar = fb % 2
        wts = [("gd", max(xn2_gd[4 * fb:4 * fb + 4])), ("mm", p3_mm), ("vc", p3_vc)]
        if xn2d_users[bpar]:
            wts.append(("mm", xn2d_users[bpar]))
        for c in range(16):
            P.op(SYN, (lambda f_, p_, c_: (lambda _nc: _nc.sync.dma_start_transpose(
                a1(XN2D[p_] + 512 * c_, 512),
                xn2_hbm[512 * f_:512 * (f_ + 1), 128 * c_:128 * (c_ + 1)])))(fb, bpar, c),
                waits=wts if c == 0 else [], incs=[("ld", 16)])
        xn2d_ldv = P.cur("ld")
        if fb == 0:
            emit_ffn_weight_loads()
        xv = a1(XN2D[bpar], 8192).rearrange("p (c k) -> p c k", k=512)
        for tt in range(4):
            t = 4 * fb + tt
            tpar = t % 2
            pm_vc = prev_midg_vc
            rcol = 2048 if tpar == 0 else 0
            mcol = 2560 if tpar == 0 else 512
            bbank = (0, 1, 2, 3) if tpar == 0 else (4, 5, 6, 7)
            wts = [("gd", x2_gd[t])]
            if t in (0, 1):
                wts.append(("vc", p3_vc))
            elif x2r_free_vc[tpar]:
                wts.append(("vc", x2r_free_vc[tpar]))
            x2r_ld = P.op(SYN, (lambda t_, p_: (lambda _nc: _nc.sync.dma_start(
                out=a2(XT[p_], 2048), in_=x2_hbm[128 * t_:128 * (t_ + 1), :])))(t, tpar),
                waits=wts, incs=[("ld", 16)])["ld"]
            wts = [("ld", xn2d_ldv), ("gd", ffnw_gd)]
            if (t - 1, 0) in evict_vc:
                wts.append(("vc", evict_vc[(t - 1, 0)]))
            for c in range(16):
                P.op(TEN, (lambda c_, t_, xvv, rc: (lambda _nc: _nc.tensor.matmul(
                    PS[:, rc:rc + 8], xvv[:, c_, 128 * t_:128 * (t_ + 1)],
                    rwt_view[:, c_, :],
                    start=(c_ == 0), stop=(c_ == 15))))(c, tt, xv, rcol),
                    waits=wts if c == 0 else [], incs=[("mm", 1)] if c == 15 else [])
            mm_r = P.cur("mm")
            wts = []
            if (t - 1, 1) in evict_vc:
                wts.append(("vc", evict_vc[(t - 1, 1)]))
            for c in range(16):
                P.op(TEN, (lambda c_, t_, xvv, mc: (lambda _nc: _nc.tensor.matmul(
                    PS[:, mc:mc + 128], xvv[:, c_, 128 * t_:128 * (t_ + 1)],
                    aall_view[:, c_, :],
                    start=(c_ == 0), stop=(c_ == 15))))(c, tt, xv, mcol),
                    waits=wts if c == 0 else [], incs=[("mm", 1)] if c == 15 else [])
            mm_m = P.cur("mm")
            v = P.op(VEC, (lambda rc: (lambda _nc: _nc.vector.tensor_add(
                glog[:, :], PS[:, rc:rc + 8], rb_sb[:, :])))(rcol),
                waits=[("mm", mm_r), ("ld", ld_p0)], incs=[("vc", 1)])["vc"]
            v = P.op(VEC, lambda _nc: _nc.vector.reduce_max(
                gnm[:, :], glog[:, :], AX.X, negate=True),
                waits=[("vc", v)], incs=[("vc", 1)])["vc"]
            a = P.op(SCA, lambda _nc: _nc.scalar.activation(
                gu[:, :], glog[:, :], AF.Exp, bias=gnm[:, 0:1]),
                waits=[("vc", v)], incs=[("ac", 1)])["ac"]
            v = P.op(VEC, lambda _nc: _nc.vector.reduce_max(
                gm1[:, :], gu[:, :], AX.X), waits=[("ac", a)], incs=[("vc", 1)])["vc"]
            v = P.op(VEC, lambda _nc: _nc.vector.tensor_scalar(
                gmk1[:, :], gu[:, :], gm1[:, 0:1], None, ALU.is_ge),
                waits=[("vc", v)], incs=[("vc", 1)])["vc"]
            v = P.op(VEC, lambda _nc: _nc.vector.scalar_tensor_tensor(
                gu2[:, :], gmk1[:, :], -1e30, gu[:, :], ALU.mult, ALU.add),
                waits=[("vc", v)], incs=[("vc", 1)])["vc"]
            v = P.op(VEC, lambda _nc: _nc.vector.reduce_max(
                gm2[:, :], gu2[:, :], AX.X), waits=[("vc", v)], incs=[("vc", 1)])["vc"]
            v = P.op(VEC, lambda _nc: _nc.vector.tensor_scalar(
                gmk2[:, :], gu[:, :], gm2[:, 0:1], None, ALU.is_ge),
                waits=[("vc", v)], incs=[("vc", 1)])["vc"]
            v = P.op(VEC, lambda _nc: _nc.vector.tensor_mul(
                ggu[:, :], gu[:, :], gmk2[:, :]), waits=[("vc", v)], incs=[("vc", 1)])["vc"]
            v = P.op(VEC, lambda _nc: _nc.vector.reduce_sum(
                gsum[:, :], ggu[:, :], AX.X), waits=[("vc", v)], incs=[("vc", 1)])["vc"]
            v = P.op(VEC, lambda _nc: _nc.vector.reciprocal(
                grec[:, :], gsum[:, :]), waits=[("vc", v)], incs=[("vc", 1)])["vc"]
            v = P.op(VEC, lambda _nc: _nc.vector.tensor_scalar(
                gate[:, :], ggu[:, :], grec[:, 0:1], LORA_SCALING, ALU.mult, ALU.mult),
                waits=[("vc", v)], incs=[("vc", 1)])["vc"]
            wts = [("vc", v), ("mm", mm_m)]
            if prev_midg_gd:
                wts.append(("gd", prev_midg_gd))
            v = P.op(VEC, (lambda mc: (lambda _nc: _nc.vector.tensor_tensor(
                a1(MIDG, 128).rearrange("p (e r) -> p e r", r=RL),
                PS[:, mc:mc + 128].rearrange("p (e r) -> p e r", r=RL),
                gate.ap().unsqueeze(2).broadcast_to([128, E, RL]), ALU.mult)))(mcol),
                waits=wts, incs=[("vc", 1)])["vc"]
            prev_midg_vc = v
            g = P.op(GPS, (lambda t_: (lambda _nc: _nc.gpsimd.dma_start(
                out=midg_hbm[128 * t_:128 * (t_ + 1), :], in_=a1(MIDG, 128))))(t),
                waits=[("vc", v)], incs=[("gd", 16)])["gd"]
            prev_midg_gd = g
            wts = [("gd", g)]
            if prev_lora_mm:
                wts.append(("mm", prev_lora_mm))
            mgld = P.op(SYN, (lambda t_: (lambda _nc: _nc.sync.dma_start_transpose(
                a1(MIDGD, 128), midg_hbm[128 * t_:128 * (t_ + 1), :])))(t),
                waits=wts, incs=[("ld", 16)])["ld"]
            oc_mm = [0] * 4
            for oc in range(4):
                wts = [("gd", bwt_gd[oc])]
                if (t - 2, oc) in evict_vc:
                    wts.append(("vc", evict_vc[(t - 2, oc)]))
                if oc == 0 and pm_vc:
                    wts.append(("vc", pm_vc))
                for c in range(16):
                    P.op(TEN, (lambda c_, o_, t_, xvv, bb_: (lambda _nc: _nc.tensor.matmul(
                        bank(bb_), xvv[:, c_, 128 * t_:128 * (t_ + 1)],
                        bwt_view[:, c_, 512 * o_:512 * (o_ + 1)],
                        start=(c_ == 0), stop=False)))(c, oc, tt, xv, bbank[oc]),
                        waits=wts if c == 0 else [])
                P.op(TEN, (lambda o_, bb_: (lambda _nc: _nc.tensor.matmul(
                    bank(bb_), a1(MIDGD, 128),
                    a1(BALL, 2048)[:, 512 * o_:512 * (o_ + 1)],
                    start=False, stop=True)))(oc, bbank[oc]),
                    waits=[("ld", mgld)] if oc == 0 else [], incs=[("mm", 1)])
                oc_mm[oc] = P.cur("mm")
            prev_lora_mm = oc_mm[3]
            for oc in range(4):
                wts = [("mm", oc_mm[oc])]
                if oc == 0:
                    wts.append(("ld", x2r_ld))
                    if t >= 2 and out_gd[t - 2]:
                        wts.append(("gd", out_gd[t - 2]))
                    if t in (0, 1):
                        wts.append(("gd", p3_gd))
                P.op(VEC, (lambda o_, p_, bb_: (lambda _nc: _nc.vector.tensor_add(
                    a2(X2T[p_], 2048)[:, 512 * o_:512 * (o_ + 1)], bank(bb_),
                    bb_sb[:, 512 * o_:512 * (o_ + 1)])))(oc, tpar, bbank[oc]),
                    waits=wts, incs=[("vc", 1)])
                vv = P.cur("vc")
                P.op(VEC, (lambda o_, p_: (lambda _nc: _nc.vector.tensor_add(
                    a2(X2T[p_], 2048)[:, 512 * o_:512 * (o_ + 1)],
                    a2(X2T[p_], 2048)[:, 512 * o_:512 * (o_ + 1)],
                    a2(XT[p_], 2048)[:, 512 * o_:512 * (o_ + 1)])))(oc, tpar),
                    waits=[("vc", vv)], incs=[("vc", 1)])
                evict_vc[(t, oc)] = P.cur("vc")
            vout = P.cur("vc")
            x2r_free_vc[tpar] = vout
            out_gd[t] = P.op(GPS, (lambda t_, p_: (lambda _nc: _nc.gpsimd.dma_start(
                out=out[128 * t_:128 * (t_ + 1), :], in_=a2(X2T[p_], 2048))))(t, tpar),
                waits=[("vc", vout)], incs=[("gd", 16)])["gd"]
        xn2d_users[bpar] = P.cur("mm")

    final_gd = P.cur("gd")
    P.op(GPS, lambda _nc: _nc.gpsimd.engine_nop(), waits=[("gd", final_gd)])

    ctx = {}
    with nc.Block() as block:
        def emit_stream(ename):
            eng = {TEN: nc.tensor, VEC: nc.vector, SCA: nc.scalar,
                   SYN: nc.sync, GPS: nc.gpsimd}[ename]
            for fn, waits, incs in P.streams[ename]:
                for s, v in waits:
                    if v > 0:
                        eng.wait_ge(sem[s], v)
                if isinstance(fn, tuple) and fn[0] == "ctx":
                    inst = fn[1](nc, ctx)
                else:
                    inst = fn(nc)
                for s, n in incs:
                    assert inst is not None
                    inst.then_inc(sem[s], n)

        @block.tensor
        def _(_):
            emit_stream(TEN)

        @block.vector
        def _(_):
            emit_stream(VEC)

        @block.scalar
        def _(_):
            emit_stream(SCA)

        @block.sync
        def _(_):
            emit_stream(SYN)

        @block.gpsimd
        def _(_):
            emit_stream(GPS)

    es.close()
    return nc


def _rope_tables():
    inv = np.exp(-math.log(10000.0) * np.arange(0, DK, 2, dtype=np.float64) / DK)
    pos = np.arange(S, dtype=np.float64)
    fr = pos[None, :] * inv[:, None]
    cos = np.cos(fr).astype(np.float32)
    sin = np.sin(fr).astype(np.float32)
    cos_d = np.repeat(cos, 2, axis=0)
    sin_star = np.empty((DK, S), np.float32)
    sin_star[0::2] = -sin
    sin_star[1::2] = sin
    return cos_d, sin_star


_NC_CACHE = []


def kernel(L0_attn_x, L0_attn_Wq, L0_attn_Wk, L0_attn_Wv, L0_attn_Wo,
           L0_ffn_base_w, L0_ffn_base_b, L0_ffn_router_w, L0_ffn_router_b,
           L0_ffn_lora_A, L0_ffn_lora_B):
    import ml_dtypes
    bf = ml_dtypes.bfloat16
    x = np.ascontiguousarray(np.asarray(L0_attn_x, dtype=np.float32))
    WqT = np.ascontiguousarray(np.asarray(L0_attn_Wq, np.float32).T)
    WkT = np.ascontiguousarray(np.asarray(L0_attn_Wk, np.float32).T)
    WvT = np.ascontiguousarray(np.asarray(L0_attn_Wv, np.float32).T)
    WoT = np.ascontiguousarray(np.asarray(L0_attn_Wo, np.float32).T)
    bwT = np.ascontiguousarray(np.asarray(L0_ffn_base_w, np.float32).T)
    bb = np.ascontiguousarray(np.broadcast_to(
        np.asarray(L0_ffn_base_b, np.float32)[None, :], (128, D)))
    rwT = np.ascontiguousarray(np.asarray(L0_ffn_router_w, np.float32).T)
    rb = np.ascontiguousarray(np.broadcast_to(
        np.asarray(L0_ffn_router_b, np.float32)[None, :], (128, E)))
    aall = np.ascontiguousarray(
        np.asarray(L0_ffn_lora_A, np.float32).transpose(1, 0, 2).reshape(D, E * RL))
    ball = np.ascontiguousarray(
        np.asarray(L0_ffn_lora_B, np.float32).reshape(E * RL, D))

    cos_d, sin_star = _rope_tables()
    ropec = np.ascontiguousarray(cos_d.astype(bf))
    ropes = np.ascontiguousarray(sin_star.astype(bf))
    kk = np.arange(128)[:, None]
    qq = np.arange(QBS)[None, :]
    dm = np.concatenate([(128 * jj + kk <= qq) for jj in range(8)],
                        axis=1).astype(np.float32)
    dmask = np.ascontiguousarray(dm.astype(bf))
    psw = np.zeros((128, 128), np.float32)
    idx = np.arange(128)
    psw[idx, idx ^ 1] = 1.0
    psw = np.ascontiguousarray(psw.astype(bf))

    if not _NC_CACHE:
        _NC_CACHE.append(build_nc())
    nc = _NC_CACHE[0]
    in_maps = []
    for c in range(8):
        b_, r_ = c // 4, c % 4
        in_maps.append({
            "x": x[b_],
            "xq": np.ascontiguousarray(x[b_][QT * r_:QT * (r_ + 1)]),
            "wqT": np.ascontiguousarray(WqT[:, 512 * r_:512 * (r_ + 1)]),
            "wkT": np.ascontiguousarray(WkT[:, 512 * r_:512 * (r_ + 1)]),
            "wvT": np.ascontiguousarray(WvT[:, 512 * r_:512 * (r_ + 1)]),
            "woT": WoT, "bwT": bwT, "bb": bb, "rwT": rwT, "rb": rb,
            "aall": aall, "ball": ball, "ropec": ropec, "ropes": ropes,
            "dmask": dmask, "pswap": psw,
            "rowoff": np.array([[QT * r_]], dtype=np.int32),
            "g3off": np.array([[(p * (S // 2) + QT * r_) if r_ < 2 else
                                (2 * S + p * (S // 2) + QT * r_ - S // 2)
                                for p in range(4)]], dtype=np.int32),
        })
    res = run_bass_kernel_spmd(nc, in_maps, core_ids=list(range(8)))
    outp = np.empty((B, S, D), np.float32)
    for c in range(8):
        b_, r_ = c // 4, c % 4
        outp[b_, QT * r_:QT * (r_ + 1), :] = res.results[c]["out"]
    return outp


# version 16
# speedup vs baseline: 1.1209x; 1.0157x over previous
import math
import contextlib
import numpy as np

import concourse.bass as bass
import concourse.mybir as mybir
from concourse.bass_utils import run_bass_kernel_spmd

F32 = mybir.dt.float32
BF16 = mybir.dt.bfloat16
I32 = mybir.dt.int32
AF = mybir.ActivationFunctionType
ALU = mybir.AluOpType
AX = mybir.AxisListType

B, S, D = 2, 8192, 2048
NH, DK = 16, 128
HL = 4
QT = 2048
NT = S
NTT = NT // 128
NFB = NT // 512
QBS = 1024
NQB = NT // QBS
E, RL = 8, 16
SCALE = 1.0 / math.sqrt(DK)
LORA_SCALING = 2.0

TEN, VEC, SCA, SYN, GPS = "tensor", "vector", "scalar", "sync", "gpsimd"


class Plan:
    def __init__(self):
        self.streams = {e: [] for e in (TEN, VEC, SCA, SYN, GPS)}
        self.ct = {}

    def op(self, eng, fn, waits=(), incs=()):
        for s, v in waits:
            have = self.ct.get(s, 0)
            assert v <= have, f"wait {s}>={v} but only {have} planned ({eng})"
        after = {}
        for s, n in incs:
            self.ct[s] = self.ct.get(s, 0) + n
            after[s] = self.ct[s]
        self.streams[eng].append((fn, tuple(waits), tuple(incs)))
        return after

    def cur(self, s):
        return self.ct.get(s, 0)


def build_nc(debug=False):
    nc = bass.Bass("TRN2", target_bir_lowering=False)

    x_in = nc.dram_tensor("x", [NT, D], F32, kind="ExternalInput").ap()
    xq_in = nc.dram_tensor("xq", [QT, D], F32, kind="ExternalInput").ap()
    wqT = nc.dram_tensor("wqT", [D, HL * DK], F32, kind="ExternalInput").ap()
    wkT = nc.dram_tensor("wkT", [D, HL * DK], F32, kind="ExternalInput").ap()
    wvT = nc.dram_tensor("wvT", [D, HL * DK], F32, kind="ExternalInput").ap()
    woT = nc.dram_tensor("woT", [D, D], F32, kind="ExternalInput").ap()
    bwT = nc.dram_tensor("bwT", [D, D], F32, kind="ExternalInput").ap()
    bb_in = nc.dram_tensor("bb", [128, D], F32, kind="ExternalInput").ap()
    rwT = nc.dram_tensor("rwT", [D, E], F32, kind="ExternalInput").ap()
    rb_in = nc.dram_tensor("rb", [128, E], F32, kind="ExternalInput").ap()
    aall = nc.dram_tensor("aall", [D, E * RL], F32, kind="ExternalInput").ap()
    ball = nc.dram_tensor("ball", [E * RL, D], F32, kind="ExternalInput").ap()
    ropec = nc.dram_tensor("ropec", [DK, NT], BF16, kind="ExternalInput").ap()
    ropes = nc.dram_tensor("ropes", [DK, NT], BF16, kind="ExternalInput").ap()
    dmask_in = nc.dram_tensor("dmask", [128, 8 * QBS], BF16, kind="ExternalInput").ap()
    pswap_in = nc.dram_tensor("pswap", [128, 128], BF16, kind="ExternalInput").ap()
    rowoff_in = nc.dram_tensor("rowoff", [1, 1], I32, kind="ExternalInput").ap()
    g3off_in = nc.dram_tensor("g3off", [1, 4], I32, kind="ExternalInput").ap()
    out = nc.dram_tensor("out", [QT, D], F32, kind="ExternalOutput").ap()

    xn_hbm = nc.dram_tensor("xn_hbm", [NT, D], BF16, kind="Internal").ap()
    q_spill = nc.dram_tensor("q_spill", [HL * DK, NT], BF16, kind="Internal").ap()
    k_spill = nc.dram_tensor("k_spill", [HL * DK, NT], BF16, kind="Internal").ap()
    v_spill = nc.dram_tensor("v_spill", [HL * NT, DK], BF16, kind="Internal").ap()
    ag_in_h = [nc.dram_tensor(f"ag_in{h}", [NT, DK], BF16, kind="Internal").ap()
               for h in range(HL)]
    ag_out_h = [nc.dram_tensor(f"ag_out{h}", [4 * NT, DK], BF16, kind="Internal").ap()
                for h in range(HL)]
    ctx_mine = nc.dram_tensor("ctx_mine", [4 * QT, HL * DK], BF16, kind="Internal").ap()
    x2_hbm = nc.dram_tensor("x2_hbm", [QT, D], F32, kind="Internal").ap()
    xn2_hbm = nc.dram_tensor("xn2_hbm", [QT, D], BF16, kind="Internal").ap()
    midg_hbm = nc.dram_tensor("midg_hbm", [QT, E * RL], BF16, kind="Internal").ap()

    P = Plan()
    es = contextlib.ExitStack()

    AR1 = es.enter_context(nc.sbuf_tensor("AR1", [128, 75776], BF16))
    AR2 = es.enter_context(nc.sbuf_tensor("AR2", [128, 8192], F32))

    def a1(off, n):
        return AR1[:, off:off + n]

    def a2(off, n):
        return AR2[:, off:off + n]

    bb_sb = es.enter_context(nc.sbuf_tensor("bb_sb", [128, D], F32))
    rb_sb = es.enter_context(nc.sbuf_tensor("rb_sb", [128, E], F32))
    pswap = es.enter_context(nc.sbuf_tensor("pswap_sb", [128, 128], BF16))
    eps_sb = es.enter_context(nc.sbuf_tensor("eps_sb", [128, 1], F32))
    stats = es.enter_context(nc.sbuf_tensor("stats", [128, 2 * 24], F32))
    mv = es.enter_context(nc.sbuf_tensor("mv", [128, 2 * 2], F32))
    lnt = es.enter_context(nc.sbuf_tensor("lnt", [128, 2], F32))
    rstd = es.enter_context(nc.sbuf_tensor("rstd", [128, 2], F32))
    mv8 = es.enter_context(nc.sbuf_tensor("mv8", [128, 2 * 16], F32))
    lnt8 = es.enter_context(nc.sbuf_tensor("lnt8", [128, 2 * 8], F32))
    rstd8 = es.enter_context(nc.sbuf_tensor("rstd8", [128, 2 * 8], F32))
    bias8 = es.enter_context(nc.sbuf_tensor("bias8", [128, 2 * 8], F32))
    recip = es.enter_context(nc.sbuf_tensor("recip", [128, 8], F32))
    glog = es.enter_context(nc.sbuf_tensor("glog", [128, E], F32))
    gnm = es.enter_context(nc.sbuf_tensor("gnm", [128, 1], F32))
    gu = es.enter_context(nc.sbuf_tensor("gu", [128, E], F32))
    gm1 = es.enter_context(nc.sbuf_tensor("gm1", [128, 1], F32))
    gmk1 = es.enter_context(nc.sbuf_tensor("gmk1", [128, E], F32))
    gu2 = es.enter_context(nc.sbuf_tensor("gu2", [128, E], F32))
    gm2 = es.enter_context(nc.sbuf_tensor("gm2", [128, 1], F32))
    gmk2 = es.enter_context(nc.sbuf_tensor("gmk2", [128, E], F32))
    ggu = es.enter_context(nc.sbuf_tensor("ggu", [128, E], F32))
    gsum = es.enter_context(nc.sbuf_tensor("gsum", [128, 1], F32))
    grec = es.enter_context(nc.sbuf_tensor("grec", [128, 1], F32))
    gate = es.enter_context(nc.sbuf_tensor("gate", [128, E], F32))
    roff_sb = es.enter_context(nc.sbuf_tensor("roff_sb", [1, 1], I32))
    g3off_sb = es.enter_context(nc.sbuf_tensor("g3off_sb", [1, 4], I32))
    dmask = es.enter_context(nc.sbuf_tensor("dmask_sb", [128, 8 * QBS], BF16))

    PS = es.enter_context(nc.psum_tensor("PS", [128, 4096], F32))

    def bank(i, n=512):
        return PS[:, 512 * i: 512 * i + n]

    sem = {}
    for name in ("ld", "gd", "mm", "vc", "ac", "cc"):
        sem[name] = es.enter_context(nc.semaphore(name))

    WQ, WK, WV = 0, 8192, 16384
    XND = (24576, 32768, 40960, 49152)
    XNT = (57344, 59392)
    QEV = (61440, 62464)
    QSW = (63488, 64512)
    RT1, RT2 = 65536, 66560
    QR = (67584, 68608)
    VST = (69632, 70144)
    RPC = (70656, 71680)
    RPS = (72704, 73728)
    KBF = (0, 8192)
    VBF = (16384, 24832)
    QBF = (33280, 34304)
    PTS = (35328, 37376)
    CTXS = (39424, 40448)
    WOT = 43008
    CTXD = 0
    XN2T = (32768, 34816)
    BWT = 43008
    XN2D = (0, 8192)
    AALL = 16384
    BALL = 18432
    RWT = 20480
    MIDG = 20608
    MIDGD = 20736
    XT = (0, 2048)
    X2T = (4096, 6144)

    for dst, src in ((pswap[:, :], pswap_in[:, :]), (bb_sb[:, :], bb_in[:, :]),
                     (rb_sb[:, :], rb_in[:, :]), (dmask[:, :], dmask_in[:, :])):
        P.op(SYN, (lambda d, s: (lambda _nc: _nc.sync.dma_start(out=d, in_=s)))(dst, src),
             incs=[("ld", 16)])
    for off, src in ((WQ, wqT), (WK, wkT), (WV, wvT)):
        dst = a1(off, 8192).rearrange("p (c k) -> p c k", k=512)
        srcr = src.rearrange("(c p) k -> p c k", p=128)
        P.op(GPS, (lambda d, s: (lambda _nc: _nc.gpsimd.dma_start(out=d, in_=s)))(dst, srcr),
             incs=[("gd", 16)])
    P.op(GPS, lambda _nc: _nc.gpsimd.dma_start(out=roff_sb[:, :], in_=rowoff_in[:, :]),
         incs=[("gd", 16)])
    P.op(GPS, lambda _nc: _nc.gpsimd.dma_start(out=g3off_sb[:, :], in_=g3off_in[:, :]),
         incs=[("gd", 16)])
    ld_p0 = P.cur("ld")
    gd_p0 = P.cur("gd")
    P.op(VEC, lambda _nc: _nc.vector.memset(eps_sb[:, :], 1e-5), incs=[("vc", 1)])

    def _mk_rowoff(_nc, _ctx):
        reg = _nc.gpsimd.alloc_register("rowoff_reg")
        _nc.gpsimd.reg_load(reg, roff_sb[0:1, 0:1])
        _ctx["rowoff"] = _nc.gpsimd.snap(reg, min_val=0, max_val=3 * QT)
        _ctx["g3off"] = []
        for p_ in range(4):
            rg = _nc.gpsimd.alloc_register(f"g3off_reg{p_}")
            _nc.gpsimd.reg_load(rg, g3off_sb[0:1, p_:p_ + 1])
            _ctx["g3off"].append(_nc.gpsimd.snap(rg, min_val=0, max_val=4 * NT - QT))
    P.op(GPS, ("ctx", _mk_rowoff), waits=[("gd", gd_p0)])

    NDB = NFB // 2
    xn_wr_gd = [0] * NTT
    xt_free = [0, 0, 0, 0]
    xn_free = [0, 0]

    def make_ln_steps(db):
        dpar = db % 2
        mv8v = mv8.ap().rearrange("p (d k two) -> p d two k", d=2, two=2)
        ld_tile = [0] * 8

        def emit_x_load(i8):
            i = 8 * db + i8
            par = i % 4
            w = [("vc", xt_free[par])] if xt_free[par] else []
            ld_tile[i8] = P.op(SYN, (lambda i_, p_: (lambda _nc: _nc.sync.dma_start(
                out=a2(2048 * p_, 2048), in_=x_in[128 * i_:128 * (i_ + 1), :])))(i, par),
                waits=w, incs=[("ld", 16)])["ld"]

        for i8 in range(4):
            emit_x_load(i8)

        def step(pr):
            for j in range(2):
                i8 = 2 * pr + j
                par = (8 * db + i8) % 4
                spar = (8 * db + i8) % 2
                for c in range(4):
                    P.op(VEC, (lambda p_, sp_, c_: (lambda _nc: _nc.vector.bn_stats(
                        stats[:, 24 * sp_ + 6 * c_:24 * sp_ + 6 * (c_ + 1)],
                        a2(2048 * p_ + 512 * c_, 512))))(par, spar, c),
                        waits=[("ld", ld_tile[i8])] if c == 0 else [],
                        incs=[("vc", 1)] if c == 3 else [])
                vbn = P.cur("vc")
                P.op(VEC, (lambda sp_, d_, k_: (lambda _nc: _nc.vector.bn_aggr(
                    mv8[:, 16 * d_ + 2 * k_:16 * d_ + 2 * k_ + 2],
                    stats[:, 24 * sp_:24 * sp_ + 24])))(spar, dpar, i8),
                    waits=[("vc", vbn)], incs=[("vc", 1)])
            vagg = P.cur("vc")
            aln = P.op(SCA, (lambda d_, r_, vv: (lambda _nc: _nc.scalar.activation(
                lnt8[:, 8 * d_ + 2 * r_:8 * d_ + 2 * r_ + 2],
                vv[:, d_, 1, 2 * r_:2 * r_ + 2], AF.Ln,
                bias=eps_sb[:, 0:1])))(dpar, pr, mv8v), waits=[("vc", vagg)],
                incs=[("ac", 1)])["ac"]
            aex = P.op(SCA, (lambda d_, r_: (lambda _nc: _nc.scalar.activation(
                rstd8[:, 8 * d_ + 2 * r_:8 * d_ + 2 * r_ + 2],
                lnt8[:, 8 * d_ + 2 * r_:8 * d_ + 2 * r_ + 2], AF.Exp,
                scale=-0.5)))(dpar, pr), waits=[("ac", aln)], incs=[("ac", 1)])["ac"]
            for j in range(2):
                i8 = 2 * pr + j
                i = 8 * db + i8
                par = i % 4
                spar = i % 2
                wts = [("ac", aex)]
                if xn_free[spar]:
                    wts.append(("gd", xn_free[spar]))
                vno = P.op(VEC, (lambda p_, sp_, d_, k_, vv: (lambda _nc:
                    _nc.vector.tensor_scalar(
                        a1(XNT[sp_], 2048), a2(2048 * p_, 2048),
                        vv[:, d_, 0, k_:k_ + 1],
                        rstd8[:, 8 * d_ + k_:8 * d_ + k_ + 1],
                        ALU.subtract, ALU.mult)))(par, spar, dpar, i8, mv8v),
                    waits=wts, incs=[("vc", 1)])["vc"]
                xt_free[par] = vno
                g = P.op(GPS, (lambda i_, sp_: (lambda _nc: _nc.gpsimd.dma_start(
                    out=xn_hbm[128 * i_:128 * (i_ + 1), :],
                    in_=a1(XNT[sp_], 2048))))(i, spar),
                    waits=[("vc", vno)], incs=[("gd", 16)])["gd"]
                xn_wr_gd[i] = g
                xn_free[spar] = g
            for j in range(2):
                if 2 * pr + 4 + j < 8:
                    emit_x_load(2 * pr + 4 + j)

        return [lambda pr=pr: step(pr) for pr in range(4)]

    xnd_rd_ld = [0] * NDB
    xnd_users_mm = [0, 0]
    rope_tab_ld = [0, 0]
    psq_free_ac = [0, 0]
    psw_free_ac = 0
    psv_free_ac = [0, 0]
    qev_free_mm = [0, 0]
    qev_free_vc = [0, 0]
    qsw_free_vc = [0, 0]
    qr_free_gd = [0, 0]
    vst_free_gd = [0, 0]
    rope_tab_users_vc = [0, 0]

    def emit_block_reads(db):
        par = db % 2
        w = [("gd", max(xn_wr_gd[8 * db + k] for k in range(8)))]
        if xnd_users_mm[par]:
            w.append(("mm", xnd_users_mm[par]))
        base = XND[2 * par]
        last = 0
        for c in range(16):
            last = P.op(SYN, (lambda d_, b_, c_: (lambda _nc: _nc.sync.dma_start_transpose(
                a1(b_ + 1024 * c_, 1024),
                xn_hbm[1024 * d_:1024 * (d_ + 1), 128 * c_:128 * (c_ + 1)])))(db, base, c),
                waits=w if c == 0 else [], incs=[("ld", 16)])["ld"]
        xnd_rd_ld[db] = last

    def emit_rope_table_loads(db):
        dpar = db % 2
        wts = []
        if rope_tab_users_vc[dpar]:
            wts.append(("gd", rope_tab_users_vc[dpar]))
        P.op(SYN, (lambda d_, p_: (lambda _nc: _nc.sync.dma_start(
            out=a1(RPC[p_], 1024), in_=ropec[:, QBS * d_:QBS * (d_ + 1)])))(db, dpar),
            waits=wts, incs=[("ld", 16)])
        rope_tab_ld[dpar] = P.op(SYN, (lambda d_, p_: (lambda _nc: _nc.sync.dma_start(
            out=a1(RPS[p_], 1024), in_=ropes[:, QBS * d_:QBS * (d_ + 1)])))(db, dpar),
            incs=[("ld", 16)])["ld"]

    def emit_proj_db(db, ln_steps):
        nonlocal psw_free_ac
        dpar = db % 2
        base = XND[2 * dpar]
        R = a1(base, 16384).rearrange("p (c b k) -> p c b k", b=2, k=512)
        rd_ld = xnd_rd_ld[db]
        jobs = [(kind, h) for h in range(HL) for kind in ("q", "k")]

        def emit_mms(ji):
            kind, h = jobs[ji]
            par = ji % 2
            woff = WQ if kind == "q" else WK
            wview = a1(woff, 8192).rearrange("p (c k) -> p c k", k=512)
            wts = [("ld", rd_ld), ("gd", gd_p0)]
            if psq_free_ac[par]:
                wts.append(("ac", psq_free_ac[par]))
            for c in range(16):
                for half in range(2):
                    P.op(TEN, (lambda p_, c_, hf, wv, h_: (lambda _nc: _nc.tensor.matmul(
                        PS[:, 1024 * p_ + 512 * hf:1024 * p_ + 512 * (hf + 1)],
                        wv[:, c_, 128 * h_:128 * (h_ + 1)], R[:, c_, hf, :],
                        start=(c_ == 0), stop=(c_ == 15))))(par, c, half, wview, h),
                        waits=wts if (c == 0 and half == 0) else [],
                        incs=[("mm", 1)] if (c == 15 and half == 1) else [])
            return P.cur("mm")

        def emit_evict(ji, mmv):
            par = ji % 2
            wts = [("mm", mmv)]
            if qev_free_mm[par]:
                wts.append(("mm", qev_free_mm[par]))
            if qev_free_vc[par]:
                wts.append(("gd", qev_free_vc[par]))
            a = P.op(SCA, (lambda p_: (lambda _nc: _nc.scalar.copy(
                a1(QEV[p_], 1024), PS[:, 1024 * p_:1024 * (p_ + 1)])))(par),
                waits=wts, incs=[("ac", 1)])["ac"]
            psq_free_ac[par] = a
            return a

        def emit_swap(ji, acv):
            nonlocal psw_free_ac
            par = ji % 2
            wts = [("ac", acv), ("ld", ld_p0)]
            if psw_free_ac:
                wts.append(("ac", psw_free_ac))
            for half in range(2):
                P.op(TEN, (lambda p_, hf: (lambda _nc: _nc.tensor.matmul(
                    PS[:, 2048 + 512 * hf:2048 + 512 * (hf + 1)], pswap[:, :],
                    a1(QEV[p_], 1024)[:, 512 * hf:512 * (hf + 1)],
                    start=True, stop=True)))(par, half),
                    waits=wts if half == 0 else [],
                    incs=[("mm", 1)] if half == 1 else [])
            mmv = P.cur("mm")
            wts = [("mm", mmv)]
            if qsw_free_vc[par]:
                wts.append(("gd", qsw_free_vc[par]))
            a = P.op(SCA, (lambda p_: (lambda _nc: _nc.scalar.copy(
                a1(QSW[p_], 1024), PS[:, 2048:3072])))(par), waits=wts,
                incs=[("ac", 1)])["ac"]
            psw_free_ac = a
            return a

        def emit_rope(ji, acv):
            kind, h = jobs[ji]
            par = ji % 2
            P.op(GPS, (lambda p_, d_: (lambda _nc: _nc.gpsimd.tensor_mul(
                a1(RT1, 1024), a1(QEV[p_], 1024), a1(RPC[d_], 1024))))(par, dpar),
                waits=[("ac", acv), ("ld", rope_tab_ld[dpar])], incs=[("gd", 1)])
            v2 = P.op(GPS, (lambda p_, d_: (lambda _nc: _nc.gpsimd.tensor_mul(
                a1(RT2, 1024), a1(QSW[p_], 1024), a1(RPS[d_], 1024))))(par, dpar),
                incs=[("gd", 1)])["gd"]
            qev_free_vc[par] = v2
            qsw_free_vc[par] = v2
            wts = [("gd", max(v2, qr_free_gd[par]))]
            v3 = P.op(GPS, (lambda p_: (lambda _nc: _nc.gpsimd.tensor_add(
                a1(QR[p_], 1024), a1(RT1, 1024), a1(RT2, 1024))))(par),
                waits=wts, incs=[("gd", 1)])["gd"]
            rope_tab_users_vc[dpar] = v3
            spill = q_spill if kind == "q" else k_spill
            g = P.op(GPS, (lambda p_, sp, h_, d_: (lambda _nc: _nc.gpsimd.dma_start(
                out=sp[128 * h_:128 * (h_ + 1), QBS * d_:QBS * (d_ + 1)],
                in_=a1(QR[p_], 1024))))(par, spill, h, db), waits=[("gd", v3)],
                incs=[("gd", 16)])["gd"]
            qr_free_gd[par] = g

        pend = []
        for ji in range(len(jobs)):
            m = emit_mms(ji)
            if pend:
                pj, pm = pend.pop()
                a = emit_evict(pj, pm)
                sa = emit_swap(pj, a)
                emit_rope(pj, sa)
            pend.append((ji, m))
            if ji % 2 == 1 and ln_steps[ji // 2] is not None:
                ln_steps[ji // 2]()
        pj, pm = pend.pop()
        a = emit_evict(pj, pm)
        sa = emit_swap(pj, a)
        emit_rope(pj, sa)

        wvview = a1(WV, 8192).rearrange("p (c k) -> p c k", k=512)
        for tt in range(8):
            par = tt % 2
            wts = [("ld", rd_ld), ("gd", gd_p0)]
            if psv_free_ac[par]:
                wts.append(("ac", psv_free_ac[par]))
            for c in range(16):
                P.op(TEN, (lambda p_, c_, t_, wv: (lambda _nc: _nc.tensor.matmul(
                    PS[:, 3072 + 512 * p_:3072 + 512 * (p_ + 1)],
                    R[:, c_, t_ // 4, 128 * (t_ % 4):128 * (t_ % 4 + 1)], wv[:, c_, :],
                    start=(c_ == 0), stop=(c_ == 15))))(par, c, tt, wvview),
                    waits=wts if c == 0 else [], incs=[("mm", 1)] if c == 15 else [])
            mmv = P.cur("mm")
            wts = [("mm", mmv)]
            if vst_free_gd[par]:
                wts.append(("gd", vst_free_gd[par]))
            av = P.op(SCA, (lambda p_: (lambda _nc: _nc.scalar.copy(
                a1(VST[p_], 512), PS[:, 3072 + 512 * p_:3072 + 512 * (p_ + 1)])))(par),
                waits=wts, incs=[("ac", 1)])["ac"]
            psv_free_ac[par] = av
            trow = QBS * db + 128 * tt
            vdst = v_spill.rearrange("(h t) d -> h t d", h=HL)[
                :, trow:trow + 128, :].rearrange("h p d -> p h d")
            vsrc = a1(VST[par], 512).rearrange("p (h d) -> p h d", d=DK)
            g = P.op(GPS, (lambda dd, ss: (lambda _nc: _nc.gpsimd.dma_start(
                out=dd, in_=ss)))(vdst, vsrc), waits=[("ac", av)], incs=[("gd", 16)])["gd"]
            vst_free_gd[par] = g
        xnd_users_mm[dpar] = P.cur("mm")

    for db0 in (0, 1):
        for s in make_ln_steps(db0):
            s()
    emit_rope_table_loads(0)
    for db in range(NDB):
        emit_block_reads(db)
        if db + 1 < NDB:
            emit_rope_table_loads(db + 1)
        steps = make_ln_steps(db + 2) if db + 2 < NDB else [None] * 4
        emit_proj_db(db, steps)

    p1_mm = P.cur("mm")
    p1_vc = P.cur("vc")
    p1_gd = P.cur("gd")


    kv_ld = [0, 0]
    kv_free_mm = [0, 0]
    vones_vc = [0, 0]

    def emit_kv_load(h):
        par = h % 2
        wts = [("gd", p1_gd)]
        if kv_free_mm[par]:
            wts.append(("mm", kv_free_mm[par]))
        P.op(SYN, (lambda p_, h_: (lambda _nc: _nc.sync.dma_start(
            out=a1(KBF[p_], 8192), in_=k_spill[128 * h_:128 * (h_ + 1), :])))(par, h),
            waits=wts, incs=[("ld", 16)])
        vdst = a1(VBF[par], 8448).rearrange("p (n c) -> p n c", c=132)[:, :, 0:128]
        vsrc = v_spill[NT * h:NT * (h + 1), :].rearrange("(n p) d -> p n d", p=128)
        P.op(SYN, (lambda dd, ss: (lambda _nc: _nc.sync.dma_start(out=dd, in_=ss)))(
            vdst, vsrc), incs=[("ld", 16)])
        kv_ld[par] = P.cur("ld")
        vcols = a1(VBF[par], 8448).rearrange("p (n c) -> p n c", c=132)[:, :, 128:129]
        vones_vc[par] = P.op(VEC, (lambda vv: (lambda _nc: _nc.vector.memset(vv, 1.0)))(
            vcols), waits=[("ld", kv_ld[par])], incs=[("vc", 1)])["vc"]

    qld = [0, 0]
    q_free_mm = [0, 0]
    pt_ps_free = [0, 0]
    pt_sb_free = [0, 0]
    ctxs_free_gd = [0, 0]
    ctx_evict_vc = 0
    chunk_ctr = 0

    emit_kv_load(0)
    P.op(GPS, lambda _nc: _nc.gpsimd.dma_start(
        out=a1(WOT, 32768).rearrange("p (c o) -> p c o", o=2048),
        in_=woT.rearrange("(c p) o -> p c o", p=128)),
        waits=[("mm", p1_mm), ("vc", p1_vc), ("ld", kv_ld[0])], incs=[("gd", 16)])
    wot_gd = P.cur("gd")

    def emit_q_load(h_, qb_):
        qpar_ = qb_ % 2
        wts = [("gd", p1_gd)]
        if q_free_mm[qpar_]:
            wts.append(("mm", q_free_mm[qpar_]))
        qld[qpar_] = P.op(SYN, (lambda p_, hh, q_: (lambda _nc: _nc.sync.dma_start(
            out=a1(QBF[p_], 1024),
            in_=q_spill[128 * hh:128 * (hh + 1), QBS * q_:QBS * (q_ + 1)])))(
            qpar_, h_, qb_), waits=wts, incs=[("ld", 16)])["ld"]

    emit_q_load(0, 0)
    for h in range(HL):
        hpar = h % 2
        if h + 1 < HL:
            emit_kv_load(h + 1)
        for qb in range(NQB):
            qpar = qb % 2
            if qb + 1 < NQB:
                emit_q_load(h, qb + 1)
            elif h + 1 < HL:
                emit_q_load(h + 1, 0)
            nch = 8 * (qb + 1)
            exp_ac = {}
            mask_vc = {}

            def emit_sT(j, nch_, qpar_, hpar_):
                nonlocal chunk_ctr
                par = chunk_ctr % 2
                wts = [("ld", max(qld[qpar_], kv_ld[hpar_])), ("vc", vones_vc[hpar_])]
                if pt_ps_free[par]:
                    wts.append(("ac", pt_ps_free[par]))
                for half in range(2):
                    P.op(TEN, (lambda p_, j_, hf, hp, qp: (lambda _nc: _nc.tensor.matmul(
                        PS[:, 1024 * p_ + 512 * hf: 1024 * p_ + 512 * (hf + 1)],
                        a1(KBF[hp], 8192)[:, 128 * j_:128 * (j_ + 1)],
                        a1(QBF[qp], 1024)[:, 512 * hf:512 * (hf + 1)],
                        start=True, stop=True)))(par, j, half, hpar_, qpar_),
                        waits=wts if half == 0 else [],
                        incs=[("mm", 1)] if half == 1 else [])
                w2 = [("mm", P.cur("mm"))]
                if pt_sb_free[par]:
                    w2.append(("mm", pt_sb_free[par]))
                a = P.op(SCA, (lambda p_: (lambda _nc: _nc.scalar.activation(
                    a1(PTS[p_], 1024), PS[:, 1024 * p_:1024 * (p_ + 1)],
                    AF.Exp, scale=SCALE)))(par), waits=w2, incs=[("ac", 1)])["ac"]
                pt_ps_free[par] = a
                exp_ac[j] = a
                if j >= nch_ - 8:
                    jj = j - (nch_ - 8)
                    v = P.op(VEC, (lambda p_, jj_: (lambda _nc: _nc.vector.tensor_mul(
                        a1(PTS[p_], 1024), a1(PTS[p_], 1024),
                        dmask[:, QBS * jj_:QBS * (jj_ + 1)])))(par, jj),
                        waits=[("ac", a), ("ld", ld_p0)], incs=[("vc", 1)])["vc"]
                    mask_vc[j] = v
                chunk_ctr += 1
                return chunk_ctr - 1

            def emit_ctx(j, gidx, nch_, hpar_):
                par = gidx % 2
                if j in mask_vc:
                    wts = [("vc", mask_vc[j])]
                else:
                    wts = [("ac", exp_ac[j])]
                if j == 0 and ctx_evict_vc:
                    wts.append(("vc", ctx_evict_vc))
                vview = a1(VBF[hpar_], 8448).rearrange("p (n c) -> p n c", c=132)
                for s_ in range(8):
                    P.op(TEN, (lambda p_, j_, ss, vv, n_: (lambda _nc: _nc.tensor.matmul(
                        PS[:, 2048 + 256 * ss: 2048 + 256 * ss + 129],
                        a1(PTS[p_], 1024)[:, 128 * ss:128 * (ss + 1)],
                        vv[:, j_, 0:129],
                        start=(j_ == 0 and ss % 2 == 0),
                        stop=(j_ == n_ - 1))))(par, j, s_, vview, nch_),
                        waits=wts if s_ == 0 else [],
                        incs=[("mm", 1)] if s_ == 7 else [])
                return P.cur("mm")

            prev = None
            for j in range(nch):
                g = emit_sT(j, nch, qpar, hpar)
                if prev is not None:
                    m = emit_ctx(prev[0], prev[1], nch, hpar)
                    pt_sb_free[prev[1] % 2] = m
                prev = (j, g)
            m = emit_ctx(prev[0], prev[1], nch, hpar)
            pt_sb_free[prev[1] % 2] = m
            ctx_done = m
            for s_ in range(8):
                P.op(VEC, (lambda ss: (lambda _nc: _nc.vector.reciprocal(
                    recip[:, ss:ss + 1],
                    PS[:, 2048 + 256 * ss + 128:2048 + 256 * ss + 129])))(s_),
                    waits=[("mm", ctx_done)] if s_ == 0 else [],
                    incs=[("vc", 1)] if s_ == 7 else [])
            vrec = P.cur("vc")
            wts = [("vc", vrec)]
            if ctxs_free_gd[qpar]:
                wts.append(("ld", ctxs_free_gd[qpar]))
            for s_ in range(8):
                P.op(VEC, (lambda p_, ss: (lambda _nc: _nc.vector.tensor_scalar_mul(
                    a1(CTXS[p_], 1024)[:, 128 * ss:128 * (ss + 1)],
                    PS[:, 2048 + 256 * ss:2048 + 256 * ss + 128],
                    recip[:, ss:ss + 1])))(qpar, s_),
                    waits=wts if s_ == 0 else [], incs=[("vc", 1)] if s_ == 7 else [])
            ctx_evict_vc = P.cur("vc")
            agdst = ag_in_h[h][QBS * qb:QBS * (qb + 1), :].rearrange(
                "(s p) d -> p s d", p=128)
            agsrc = a1(CTXS[qpar], 1024).rearrange("p (s d) -> p s d", d=128)
            g2 = P.op(SYN, (lambda dd, ss: (lambda _nc: _nc.sync.dma_start(
                out=dd, in_=ss)))(agdst, agsrc), waits=[("vc", ctx_evict_vc)],
                incs=[("ld", 16)])["ld"]
            ctxs_free_gd[qpar] = g2
            q_free_mm[qpar] = ctx_done
            if h == 3 and qb == 3:
                P.op(GPS, lambda _nc: _nc.gpsimd.collective_compute(
                    "AllGather", ALU.bypass, ins=[ag_in_h[3][0:NT // 2, :].opt()],
                    outs=[ag_out_h[3][0:2 * NT, :].opt()],
                    replica_groups=[[0, 1, 2, 3], [4, 5, 6, 7]]),
                    waits=[("gd", P.cur("gd")), ("ld", P.cur("ld"))], incs=[("cc", 1)])
        if h < 3:
            P.op(GPS, (lambda h_: (lambda _nc: _nc.gpsimd.collective_compute(
                "AllGather", ALU.bypass, ins=[ag_in_h[h_].opt()],
                outs=[ag_out_h[h_].opt()],
                replica_groups=[[0, 1, 2, 3], [4, 5, 6, 7]])))(h),
                waits=[("gd", P.cur("gd")), ("ld", P.cur("ld"))], incs=[("cc", 1)])
            cch = P.cur("cc")
            for p_ in range(4):
                def _mk_gather(pp, hh):
                    def f(_nc, _ctx):
                        v = _ctx["rowoff"]
                        return _nc.gpsimd.dma_start(
                            out=ctx_mine[QT * pp:QT * (pp + 1), 128 * hh:128 * (hh + 1)],
                            in_=ag_out_h[hh][NT * pp:NT * (pp + 1), :][bass.ds(v, QT), :])
                    return f
                P.op(GPS, ("ctx", _mk_gather(p_, h)),
                     waits=[("cc", cch)] if p_ == 0 else [], incs=[("gd", 16)])
        else:
            P.op(GPS, lambda _nc: _nc.gpsimd.collective_compute(
                "AllGather", ALU.bypass, ins=[ag_in_h[3][NT // 2:NT, :].opt()],
                outs=[ag_out_h[3][2 * NT:4 * NT, :].opt()],
                replica_groups=[[0, 1, 2, 3], [4, 5, 6, 7]]),
                waits=[("gd", P.cur("gd")), ("ld", P.cur("ld"))], incs=[("cc", 1)])
            cch = P.cur("cc")
            for p_ in range(4):
                def _mk_gather3(pp):
                    def f(_nc, _ctx):
                        v = _ctx["g3off"][pp]
                        return _nc.gpsimd.dma_start(
                            out=ctx_mine[QT * pp:QT * (pp + 1), 384:512],
                            in_=ag_out_h[3][bass.ds(v, QT), :])
                    return f
                P.op(GPS, ("ctx", _mk_gather3(p_)),
                     waits=[("cc", cch)] if p_ == 0 else [], incs=[("gd", 16)])
        kv_free_mm[hpar] = P.cur("mm")

    p2_gd = P.cur("gd")
    p2_mm = P.cur("mm")
    p2_vc = P.cur("vc")
    gather_gd = P.cur("gd")
    for ci in range(16):
        pq, hl_ = ci // 4, ci % 4
        P.op(SYN, (lambda p_, h_, c_: (lambda _nc: _nc.sync.dma_start_transpose(
            a1(CTXD + 2048 * c_, 2048),
            ctx_mine[QT * p_:QT * (p_ + 1), 128 * h_:128 * (h_ + 1)])))(pq, hl_, ci),
            waits=[("gd", gather_gd), ("mm", p2_mm), ("vc", p2_vc)] if ci == 0 else [],
            incs=[("ld", 16)])
    ctxd_ld = P.cur("ld")

    xq_free_vc = [0, 0]
    x2_free = [0, 0]
    op_ps_free = [0, 0]
    xn2_gd = [0] * 16
    x2_gd = [0] * 16

    wot_view = a1(WOT, 32768).rearrange("p (c o) -> p c o", o=2048)
    ctxd_view = a1(CTXD, 32768).rearrange("p (c t) -> p c t", t=2048)

    for t in range(16):
        par = t % 2
        wts = []
        if xq_free_vc[par]:
            wts.append(("vc", xq_free_vc[par]))
        xq_ld = P.op(SYN, (lambda t_, p_: (lambda _nc: _nc.sync.dma_start(
            out=a2(XT[p_], 2048), in_=xq_in[128 * t_:128 * (t_ + 1), :])))(t, par),
            waits=wts, incs=[("ld", 16)])["ld"]
        for oc in range(4):
            bk = 4 * par + oc
            wts = [("ld", ctxd_ld), ("gd", wot_gd)]
            if op_ps_free[par]:
                wts.append(("vc", op_ps_free[par]))
            for hdc in range(16):
                P.op(TEN, (lambda b_, h_, o_, t_: (lambda _nc: _nc.tensor.matmul(
                    bank(b_), ctxd_view[:, h_, 128 * t_:128 * (t_ + 1)],
                    wot_view[:, h_, 512 * o_:512 * (o_ + 1)],
                    start=(h_ == 0), stop=(h_ == 15))))(bk, hdc, oc, t),
                    waits=wts if (hdc == 0 and oc == 0) else [],
                    incs=[("mm", 1)] if hdc == 15 else [])
        mmv = P.cur("mm")
        wts = [("mm", mmv), ("ld", xq_ld)]
        if x2_free[par]:
            wts.append(("gd", x2_free[par]))
        for oc in range(4):
            bk = 4 * par + oc
            P.op(VEC, (lambda b_, p_, o_: (lambda _nc: _nc.vector.tensor_add(
                a2(X2T[p_], 2048)[:, 512 * o_:512 * (o_ + 1)], bank(b_),
                a2(XT[p_], 2048)[:, 512 * o_:512 * (o_ + 1)])))(bk, par, oc),
                waits=wts if oc == 0 else [], incs=[("vc", 1)] if oc == 3 else [])
        vadd = P.cur("vc")
        op_ps_free[par] = vadd
        xq_free_vc[par] = vadd
        for c in range(4):
            P.op(VEC, (lambda p_, c_: (lambda _nc: _nc.vector.bn_stats(
                stats[:, 24 * p_ + 6 * c_:24 * p_ + 6 * (c_ + 1)],
                a2(X2T[p_] + 512 * c_, 512))))(par, c),
                waits=[("vc", vadd)] if c == 0 else [],
                incs=[("vc", 1)] if c == 3 else [])
        vbn = P.cur("vc")
        vag = P.op(VEC, (lambda p_: (lambda _nc: _nc.vector.bn_aggr(
            mv[:, 2 * p_:2 * p_ + 2], stats[:, 24 * p_:24 * p_ + 24])))(par),
            waits=[("vc", vbn)], incs=[("vc", 1)])["vc"]
        aln = P.op(SCA, (lambda p_: (lambda _nc: _nc.scalar.activation(
            lnt[:, p_:p_ + 1], mv[:, 2 * p_ + 1:2 * p_ + 2], AF.Ln,
            bias=eps_sb[:, 0:1])))(par), waits=[("vc", vag)], incs=[("ac", 1)])["ac"]
        aex = P.op(SCA, (lambda p_: (lambda _nc: _nc.scalar.activation(
            rstd[:, p_:p_ + 1], lnt[:, p_:p_ + 1], AF.Exp, scale=-0.5)))(par),
            waits=[("ac", aln)], incs=[("ac", 1)])["ac"]
        vno = P.op(VEC, (lambda p_: (lambda _nc: _nc.vector.tensor_scalar(
            a1(XN2T[p_], 2048), a2(X2T[p_], 2048), mv[:, 2 * p_:2 * p_ + 1],
            rstd[:, p_:p_ + 1], ALU.subtract, ALU.mult)))(par),
            waits=[("ac", aex)], incs=[("vc", 1)])["vc"]
        P.op(GPS, (lambda t_, p_: (lambda _nc: _nc.gpsimd.dma_start(
            out=x2_hbm[128 * t_:128 * (t_ + 1), :], in_=a2(X2T[p_], 2048))))(t, par),
            waits=[("vc", vno)], incs=[("gd", 16)])
        g = P.op(GPS, (lambda t_, p_: (lambda _nc: _nc.gpsimd.dma_start(
            out=xn2_hbm[128 * t_:128 * (t_ + 1), :], in_=a1(XN2T[p_], 2048))))(t, par),
            incs=[("gd", 16)])["gd"]
        xn2_gd[t] = g
        x2_gd[t] = g
        x2_free[par] = g

    p3_mm = P.cur("mm")
    p3_vc = P.cur("vc")
    p3_gd = P.cur("gd")

    bwt_gd = [0] * 4
    ffnw_gd = 0

    def emit_ffn_weight_loads():
        nonlocal bwt_gd, ffnw_gd
        P.op(GPS, lambda _nc: _nc.gpsimd.dma_start(
            out=a1(RWT, 128).rearrange("p (c k) -> p c k", k=8),
            in_=rwT.rearrange("(c p) k -> p c k", p=128)),
            waits=[("mm", p3_mm), ("vc", p3_vc)], incs=[("gd", 16)])
        P.op(GPS, lambda _nc: _nc.gpsimd.dma_start(
            out=a1(AALL, 2048).rearrange("p (c k) -> p c k", k=128),
            in_=aall.rearrange("(c p) k -> p c k", p=128)), incs=[("gd", 16)])
        P.op(GPS, lambda _nc: _nc.gpsimd.dma_start(
            out=a1(BALL, 2048), in_=ball[:, :]), incs=[("gd", 16)])
        ffnw_gd = P.cur("gd")
        for oc in range(4):
            P.op(GPS, (lambda o_: (lambda _nc: _nc.gpsimd.dma_start(
                out=a1(BWT, 32768).rearrange("p (c o) -> p c o", o=2048)[:, :, 512 * o_:512 * (o_ + 1)],
                in_=bwT.rearrange("(c p) o -> p c o", p=128)[:, :, 512 * o_:512 * (o_ + 1)])))(oc),
                incs=[("gd", 16)])
            bwt_gd[oc] = P.cur("gd")

    rwt_view = a1(RWT, 128).rearrange("p (c k) -> p c k", k=8)
    aall_view = a1(AALL, 2048).rearrange("p (c k) -> p c k", k=128)
    bwt_view = a1(BWT, 32768).rearrange("p (c o) -> p c o", o=2048)

    xn2d_users = [0, 0]
    prev_midg_vc = 0
    prev_midg_gd = 0
    prev_lora_mm = 0
    evict_vc = {}
    out_gd = [0] * 16
    x2r_free_vc = [0, 0]

    for fb in range(4):
        bpar = fb % 2
        wts = [("gd", max(xn2_gd[4 * fb:4 * fb + 4])), ("mm", p3_mm), ("vc", p3_vc)]
        if xn2d_users[bpar]:
            wts.append(("mm", xn2d_users[bpar]))
        for c in range(16):
            P.op(SYN, (lambda f_, p_, c_: (lambda _nc: _nc.sync.dma_start_transpose(
                a1(XN2D[p_] + 512 * c_, 512),
                xn2_hbm[512 * f_:512 * (f_ + 1), 128 * c_:128 * (c_ + 1)])))(fb, bpar, c),
                waits=wts if c == 0 else [], incs=[("ld", 16)])
        xn2d_ldv = P.cur("ld")
        if fb == 0:
            emit_ffn_weight_loads()
        xv = a1(XN2D[bpar], 8192).rearrange("p (c k) -> p c k", k=512)
        for tt in range(4):
            t = 4 * fb + tt
            tpar = t % 2
            pm_vc = prev_midg_vc
            rcol = 2048 if tpar == 0 else 0
            mcol = 2560 if tpar == 0 else 512
            bbank = (0, 1, 2, 3) if tpar == 0 else (4, 5, 6, 7)
            wts = [("gd", x2_gd[t])]
            if t in (0, 1):
                wts.append(("vc", p3_vc))
            elif x2r_free_vc[tpar]:
                wts.append(("vc", x2r_free_vc[tpar]))
            x2r_ld = P.op(SYN, (lambda t_, p_: (lambda _nc: _nc.sync.dma_start(
                out=a2(XT[p_], 2048), in_=x2_hbm[128 * t_:128 * (t_ + 1), :])))(t, tpar),
                waits=wts, incs=[("ld", 16)])["ld"]
            wts = [("ld", xn2d_ldv), ("gd", ffnw_gd)]
            if (t - 1, 0) in evict_vc:
                wts.append(("vc", evict_vc[(t - 1, 0)]))
            for c in range(16):
                P.op(TEN, (lambda c_, t_, xvv, rc: (lambda _nc: _nc.tensor.matmul(
                    PS[:, rc:rc + 8], xvv[:, c_, 128 * t_:128 * (t_ + 1)],
                    rwt_view[:, c_, :],
                    start=(c_ == 0), stop=(c_ == 15))))(c, tt, xv, rcol),
                    waits=wts if c == 0 else [], incs=[("mm", 1)] if c == 15 else [])
            mm_r = P.cur("mm")
            wts = []
            if (t - 1, 1) in evict_vc:
                wts.append(("vc", evict_vc[(t - 1, 1)]))
            for c in range(16):
                P.op(TEN, (lambda c_, t_, xvv, mc: (lambda _nc: _nc.tensor.matmul(
                    PS[:, mc:mc + 128], xvv[:, c_, 128 * t_:128 * (t_ + 1)],
                    aall_view[:, c_, :],
                    start=(c_ == 0), stop=(c_ == 15))))(c, tt, xv, mcol),
                    waits=wts if c == 0 else [], incs=[("mm", 1)] if c == 15 else [])
            mm_m = P.cur("mm")
            v = P.op(VEC, (lambda rc: (lambda _nc: _nc.vector.tensor_add(
                glog[:, :], PS[:, rc:rc + 8], rb_sb[:, :])))(rcol),
                waits=[("mm", mm_r), ("ld", ld_p0)], incs=[("vc", 1)])["vc"]
            v = P.op(VEC, lambda _nc: _nc.vector.reduce_max(
                gnm[:, :], glog[:, :], AX.X, negate=True),
                waits=[("vc", v)], incs=[("vc", 1)])["vc"]
            a = P.op(SCA, lambda _nc: _nc.scalar.activation(
                gu[:, :], glog[:, :], AF.Exp, bias=gnm[:, 0:1]),
                waits=[("vc", v)], incs=[("ac", 1)])["ac"]
            v = P.op(VEC, lambda _nc: _nc.vector.reduce_max(
                gm1[:, :], gu[:, :], AX.X), waits=[("ac", a)], incs=[("vc", 1)])["vc"]
            v = P.op(VEC, lambda _nc: _nc.vector.tensor_scalar(
                gmk1[:, :], gu[:, :], gm1[:, 0:1], None, ALU.is_ge),
                waits=[("vc", v)], incs=[("vc", 1)])["vc"]
            v = P.op(VEC, lambda _nc: _nc.vector.scalar_tensor_tensor(
                gu2[:, :], gmk1[:, :], -1e30, gu[:, :], ALU.mult, ALU.add),
                waits=[("vc", v)], incs=[("vc", 1)])["vc"]
            v = P.op(VEC, lambda _nc: _nc.vector.reduce_max(
                gm2[:, :], gu2[:, :], AX.X), waits=[("vc", v)], incs=[("vc", 1)])["vc"]
            v = P.op(VEC, lambda _nc: _nc.vector.tensor_scalar(
                gmk2[:, :], gu[:, :], gm2[:, 0:1], None, ALU.is_ge),
                waits=[("vc", v)], incs=[("vc", 1)])["vc"]
            v = P.op(VEC, lambda _nc: _nc.vector.tensor_mul(
                ggu[:, :], gu[:, :], gmk2[:, :]), waits=[("vc", v)], incs=[("vc", 1)])["vc"]
            v = P.op(VEC, lambda _nc: _nc.vector.reduce_sum(
                gsum[:, :], ggu[:, :], AX.X), waits=[("vc", v)], incs=[("vc", 1)])["vc"]
            v = P.op(VEC, lambda _nc: _nc.vector.reciprocal(
                grec[:, :], gsum[:, :]), waits=[("vc", v)], incs=[("vc", 1)])["vc"]
            v = P.op(VEC, lambda _nc: _nc.vector.tensor_scalar(
                gate[:, :], ggu[:, :], grec[:, 0:1], LORA_SCALING, ALU.mult, ALU.mult),
                waits=[("vc", v)], incs=[("vc", 1)])["vc"]
            wts = [("vc", v), ("mm", mm_m)]
            if prev_midg_gd:
                wts.append(("gd", prev_midg_gd))
            v = P.op(VEC, (lambda mc: (lambda _nc: _nc.vector.tensor_tensor(
                a1(MIDG, 128).rearrange("p (e r) -> p e r", r=RL),
                PS[:, mc:mc + 128].rearrange("p (e r) -> p e r", r=RL),
                gate.ap().unsqueeze(2).broadcast_to([128, E, RL]), ALU.mult)))(mcol),
                waits=wts, incs=[("vc", 1)])["vc"]
            prev_midg_vc = v
            g = P.op(GPS, (lambda t_: (lambda _nc: _nc.gpsimd.dma_start(
                out=midg_hbm[128 * t_:128 * (t_ + 1), :], in_=a1(MIDG, 128))))(t),
                waits=[("vc", v)], incs=[("gd", 16)])["gd"]
            prev_midg_gd = g
            wts = [("gd", g)]
            if prev_lora_mm:
                wts.append(("mm", prev_lora_mm))
            mgld = P.op(SYN, (lambda t_: (lambda _nc: _nc.sync.dma_start_transpose(
                a1(MIDGD, 128), midg_hbm[128 * t_:128 * (t_ + 1), :])))(t),
                waits=wts, incs=[("ld", 16)])["ld"]
            oc_mm = [0] * 4
            for oc in range(4):
                wts = [("gd", bwt_gd[oc])]
                if (t - 2, oc) in evict_vc:
                    wts.append(("vc", evict_vc[(t - 2, oc)]))
                if oc == 0 and pm_vc:
                    wts.append(("vc", pm_vc))
                for c in range(16):
                    P.op(TEN, (lambda c_, o_, t_, xvv, bb_: (lambda _nc: _nc.tensor.matmul(
                        bank(bb_), xvv[:, c_, 128 * t_:128 * (t_ + 1)],
                        bwt_view[:, c_, 512 * o_:512 * (o_ + 1)],
                        start=(c_ == 0), stop=False)))(c, oc, tt, xv, bbank[oc]),
                        waits=wts if c == 0 else [])
                P.op(TEN, (lambda o_, bb_: (lambda _nc: _nc.tensor.matmul(
                    bank(bb_), a1(MIDGD, 128),
                    a1(BALL, 2048)[:, 512 * o_:512 * (o_ + 1)],
                    start=False, stop=True)))(oc, bbank[oc]),
                    waits=[("ld", mgld)] if oc == 0 else [], incs=[("mm", 1)])
                oc_mm[oc] = P.cur("mm")
            prev_lora_mm = oc_mm[3]
            for oc in range(4):
                wts = [("mm", oc_mm[oc])]
                if oc == 0:
                    wts.append(("ld", x2r_ld))
                    if t >= 2 and out_gd[t - 2]:
                        wts.append(("gd", out_gd[t - 2]))
                    if t in (0, 1):
                        wts.append(("gd", p3_gd))
                P.op(VEC, (lambda o_, p_, bb_: (lambda _nc: _nc.vector.tensor_add(
                    a2(X2T[p_], 2048)[:, 512 * o_:512 * (o_ + 1)], bank(bb_),
                    bb_sb[:, 512 * o_:512 * (o_ + 1)])))(oc, tpar, bbank[oc]),
                    waits=wts, incs=[("vc", 1)])
                vv = P.cur("vc")
                P.op(VEC, (lambda o_, p_: (lambda _nc: _nc.vector.tensor_add(
                    a2(X2T[p_], 2048)[:, 512 * o_:512 * (o_ + 1)],
                    a2(X2T[p_], 2048)[:, 512 * o_:512 * (o_ + 1)],
                    a2(XT[p_], 2048)[:, 512 * o_:512 * (o_ + 1)])))(oc, tpar),
                    waits=[("vc", vv)], incs=[("vc", 1)])
                evict_vc[(t, oc)] = P.cur("vc")
            vout = P.cur("vc")
            x2r_free_vc[tpar] = vout
            out_gd[t] = P.op(GPS, (lambda t_, p_: (lambda _nc: _nc.gpsimd.dma_start(
                out=out[128 * t_:128 * (t_ + 1), :], in_=a2(X2T[p_], 2048))))(t, tpar),
                waits=[("vc", vout)], incs=[("gd", 16)])["gd"]
        xn2d_users[bpar] = P.cur("mm")

    final_gd = P.cur("gd")
    P.op(GPS, lambda _nc: _nc.gpsimd.engine_nop(), waits=[("gd", final_gd)])

    ctx = {}
    with nc.Block() as block:
        def emit_stream(ename):
            eng = {TEN: nc.tensor, VEC: nc.vector, SCA: nc.scalar,
                   SYN: nc.sync, GPS: nc.gpsimd}[ename]
            for fn, waits, incs in P.streams[ename]:
                for s, v in waits:
                    if v > 0:
                        eng.wait_ge(sem[s], v)
                if isinstance(fn, tuple) and fn[0] == "ctx":
                    inst = fn[1](nc, ctx)
                else:
                    inst = fn(nc)
                for s, n in incs:
                    assert inst is not None
                    inst.then_inc(sem[s], n)

        @block.tensor
        def _(_):
            emit_stream(TEN)

        @block.vector
        def _(_):
            emit_stream(VEC)

        @block.scalar
        def _(_):
            emit_stream(SCA)

        @block.sync
        def _(_):
            emit_stream(SYN)

        @block.gpsimd
        def _(_):
            emit_stream(GPS)

    es.close()
    return nc


def _rope_tables():
    inv = np.exp(-math.log(10000.0) * np.arange(0, DK, 2, dtype=np.float64) / DK)
    pos = np.arange(S, dtype=np.float64)
    fr = pos[None, :] * inv[:, None]
    cos = np.cos(fr).astype(np.float32)
    sin = np.sin(fr).astype(np.float32)
    cos_d = np.repeat(cos, 2, axis=0)
    sin_star = np.empty((DK, S), np.float32)
    sin_star[0::2] = -sin
    sin_star[1::2] = sin
    return cos_d, sin_star


_NC_CACHE = []


def kernel(L0_attn_x, L0_attn_Wq, L0_attn_Wk, L0_attn_Wv, L0_attn_Wo,
           L0_ffn_base_w, L0_ffn_base_b, L0_ffn_router_w, L0_ffn_router_b,
           L0_ffn_lora_A, L0_ffn_lora_B):
    import ml_dtypes
    bf = ml_dtypes.bfloat16
    x = np.ascontiguousarray(np.asarray(L0_attn_x, dtype=np.float32))
    WqT = np.ascontiguousarray(np.asarray(L0_attn_Wq, np.float32).T)
    WkT = np.ascontiguousarray(np.asarray(L0_attn_Wk, np.float32).T)
    WvT = np.ascontiguousarray(np.asarray(L0_attn_Wv, np.float32).T)
    WoT = np.ascontiguousarray(np.asarray(L0_attn_Wo, np.float32).T)
    bwT = np.ascontiguousarray(np.asarray(L0_ffn_base_w, np.float32).T)
    bb = np.ascontiguousarray(np.broadcast_to(
        np.asarray(L0_ffn_base_b, np.float32)[None, :], (128, D)))
    rwT = np.ascontiguousarray(np.asarray(L0_ffn_router_w, np.float32).T)
    rb = np.ascontiguousarray(np.broadcast_to(
        np.asarray(L0_ffn_router_b, np.float32)[None, :], (128, E)))
    aall = np.ascontiguousarray(
        np.asarray(L0_ffn_lora_A, np.float32).transpose(1, 0, 2).reshape(D, E * RL))
    ball = np.ascontiguousarray(
        np.asarray(L0_ffn_lora_B, np.float32).reshape(E * RL, D))

    cos_d, sin_star = _rope_tables()
    ropec = np.ascontiguousarray(cos_d.astype(bf))
    ropes = np.ascontiguousarray(sin_star.astype(bf))
    kk = np.arange(128)[:, None]
    qq = np.arange(QBS)[None, :]
    dm = np.concatenate([(128 * jj + kk <= qq) for jj in range(8)],
                        axis=1).astype(np.float32)
    dmask = np.ascontiguousarray(dm.astype(bf))
    psw = np.zeros((128, 128), np.float32)
    idx = np.arange(128)
    psw[idx, idx ^ 1] = 1.0
    psw = np.ascontiguousarray(psw.astype(bf))

    if not _NC_CACHE:
        _NC_CACHE.append(build_nc())
    nc = _NC_CACHE[0]
    in_maps = []
    for c in range(8):
        b_, r_ = c // 4, c % 4
        in_maps.append({
            "x": x[b_],
            "xq": np.ascontiguousarray(x[b_][QT * r_:QT * (r_ + 1)]),
            "wqT": np.ascontiguousarray(WqT[:, 512 * r_:512 * (r_ + 1)]),
            "wkT": np.ascontiguousarray(WkT[:, 512 * r_:512 * (r_ + 1)]),
            "wvT": np.ascontiguousarray(WvT[:, 512 * r_:512 * (r_ + 1)]),
            "woT": WoT, "bwT": bwT, "bb": bb, "rwT": rwT, "rb": rb,
            "aall": aall, "ball": ball, "ropec": ropec, "ropes": ropes,
            "dmask": dmask, "pswap": psw,
            "rowoff": np.array([[QT * r_]], dtype=np.int32),
            "g3off": np.array([[(p * (S // 2) + QT * r_) if r_ < 2 else
                                (2 * S + p * (S // 2) + QT * r_ - S // 2)
                                for p in range(4)]], dtype=np.int32),
        })
    res = run_bass_kernel_spmd(nc, in_maps, core_ids=list(range(8)))
    outp = np.empty((B, S, D), np.float32)
    for c in range(8):
        b_, r_ = c // 4, c % 4
        outp[b_, QT * r_:QT * (r_ + 1), :] = res.results[c]["out"]
    return outp
